# version 06
# speedup vs baseline: 1.6018x; 1.2466x over previous
import contextlib
import numpy as np
import ml_dtypes
import bass_rust
import concourse.bass as bass
import concourse.mybir as mybir
from concourse.bass_utils import run_bass_kernel_spmd

F32 = mybir.dt.float32
BF16 = mybir.dt.bfloat16
AF = mybir.ActivationFunctionType
ALU = mybir.AluOpType

D = 1024
T = 4096
NTT = T // 128
NKT = D // 128
NS = T // 512
EPS_LN = 1e-5


class Cnt:
    def __init__(self, sem):
        self.sem = sem
        self.v = 0

    def inc(self, n=1):
        self.v += n
        return self.v


class Prog:
    ENGS = ("sync", "gpsimd", "vector", "scalar", "tensor")

    def __init__(self, nc):
        self.nc = nc
        self.q = {e: [] for e in self.ENGS}
        self.dirty = {e: set() for e in self.ENGS}

    def _names(self, aps):
        out = []
        for a in aps or []:
            try:
                out.append(a.tensor.name)
            except Exception:
                pass
        return out

    def emit(self, eng, fn, reads=None, writes=None):
        if eng in ("vector", "scalar"):
            rn = set(self._names(reads))
            if rn & self.dirty[eng]:
                self.q[eng].append(lambda h: h.drain())
                self.dirty[eng] = set()
            for n in self._names(writes):
                self.dirty[eng].add(n)
        self.q[eng].append(fn)

    def finalize(self):
        with self.nc.Block() as block:
            for eng in self.ENGS:
                ops = list(self.q[eng])
                if not ops:
                    continue

                def make(ops):
                    def body(h):
                        for f in ops:
                            f(h)
                    return body
                getattr(block, eng)(make(ops))


class Build:
    def __init__(self, name):
        self.nc = bass.Bass("TRN2", target_bir_lowering=False)
        self.stack = contextlib.ExitStack()
        self.p = Prog(self.nc)
        self.sems = {}
        self.pref = ""
        self.pstack = None
        self.fused = False
        self.phase = 0
        self.ph = None
        self.pdone = None
        self.ldc = None
        self.io_map = {}

    def _stk(self):
        return self.pstack if self.pstack is not None else self.stack

    def sem(self, name):
        name = self.pref + name
        if name not in self.sems:
            h = self._stk().enter_context(self.nc.semaphore(name))
            self.sems[name] = Cnt(h)
        return self.sems[name]

    def sb(self, name, shape, dtype):
        return self._stk().enter_context(
            self.nc.sbuf_tensor(self.pref + name, shape, dtype))

    def dram_in(self, name, shape, dtype=F32):
        if name in self.io_map:
            return self.io_map[name]
        return self.nc.dram_tensor(self.pref + name, shape, dtype,
                                   kind="ExternalInput")

    def dram_out(self, name, shape, dtype=F32):
        if name in self.io_map:
            return self.io_map[name]
        return self.nc.dram_tensor(self.pref + name, shape, dtype,
                                   kind="ExternalOutput")

    def psum_arena(self):
        self.psA = self.stack.enter_context(
            self.nc.psum_tensor("psA", [128, 4096], F32))

    def bank(self, i, w=512):
        return self.psA[:, 512 * i: 512 * i + w]

    def bank_bf(self, i, w=1024):
        return self.psA[:].bitcast(BF16)[:, 1024 * i: 1024 * i + w]


def common_consts(b: Build):
    b.ones_col = b.sb("ones_col", [128, 1], BF16)
    b.ones_row = b.sb("ones_row", [1, 128], BF16)
    b.ones_row_f = b.sb("ones_row_f", [1, 128], F32)
    b.ident = b.sb("ident", [128, 128], BF16)
    c = b.dram_in("consts", [128, 258], F32)
    ld = b.sem("ld_const")

    def f(g):
        g.dma_start(out=b.ones_col[:], in_=c.ap()[:, 0:1],
                    allow_slow_non_contiguous=True).then_inc(ld.sem, 16)
        g.dma_start(out=b.ident[:], in_=c.ap()[:, 1:129],
                    allow_slow_non_contiguous=True).then_inc(ld.sem, 16)
        g.dma_start(out=b.ones_row[:], in_=c.ap()[0:1, 129:257],
                    allow_slow_non_contiguous=True).then_inc(ld.sem, 16)
        g.dma_start(out=b.ones_row_f[:], in_=c.ap()[0:1, 129:257],
                    allow_slow_non_contiguous=True).then_inc(ld.sem, 16)
    b.p.emit("gpsimd", f)
    ld.inc(64)
    b.ldc = ld
    return ld


def host_consts():
    c = np.zeros((128, 258), np.float32)
    c[:, 0] = 1.0
    c[:, 1:129] = np.eye(128)
    c[0, 129:257] = 1.0
    return c


def make_postln(b, r_bf, out_hbm, st, r_ready, rval_fn, stat_banks=(2, 3),
                pref="", rcol=None, bufs=None):
    p = b.p
    if rcol is None:
        rcol = lambda kt, tt2: kt * T + tt2 * 512
    bA, bB = stat_banks
    bufs = bufs if bufs is not None else {}
    sq = bufs["sq"] if "sq" in bufs else b.sb(pref + "sq", [128, 2 * 512], BF16)
    mu_row = bufs["mu_row"] if "mu_row" in bufs else b.sb(pref + "mu_row", [1, 512], F32)
    rstd_row = (bufs["rstd_row"] if "rstd_row" in bufs
                else b.sb(pref + "rstd_row", [1, 512], F32))
    tmp_row = b.sb(pref + "tmp_row", [1, 512], F32)
    xa_f = bufs["xa_f"] if "xa_f" in bufs else b.sb(pref + "xa_f", [128, 2 * 512], F32)
    sq_done = b.sem(pref + "sq_done")
    sq_cons = b.sem(pref + "sq_cons")
    stat_fin = b.sem(pref + "stat_fin")
    bc_done = stat_fin
    ap_done = b.sem(pref + "ap_done")

    def emit_slice(tt2):
        rv = rval_fn(tt2)
        for kt in range(NKT):
            gi = tt2 * NKT + kt

            def fsq(s, kt=kt, tt2=tt2, gi=gi, rv=rv):
                if kt == 0:
                    s.wait_ge(r_ready.sem, rv)
                if gi >= 2:
                    s.wait_ge(sq_cons.sem, gi - 1)
                c0 = rcol(kt, tt2)
                s.activation(out=sq[:, (gi % 2) * 512:(gi % 2) * 512 + 512],
                             in_=r_bf[:, c0:c0 + 512],
                             func=AF.Square).then_inc(sq_done.sem, 1)
            p.emit("scalar", fsq, reads=[r_bf[:]], writes=[sq[:]])
            sqv = sq_done.inc()

            def fmm(t, kt=kt, tt2=tt2, gi=gi, rv=rv, sqv=sqv):
                if kt == 0:
                    t.wait_ge(r_ready.sem, rv)
                    if tt2 > 0:
                        t.wait_ge(ap_done.sem, tt2 * NKT)
                t.wait_ge(sq_done.sem, sqv)
                t.matmul(b.bank(bA)[0:1, :], b.ones_col[:],
                         r_bf[:, rcol(kt, tt2):rcol(kt, tt2) + 512],
                         start=(kt == 0), stop=(kt == NKT - 1))
                mm = t.matmul(b.bank(bB)[0:1, :], b.ones_col[:],
                              sq[:, (gi % 2) * 512:(gi % 2) * 512 + 512],
                              start=(kt == 0), stop=(kt == NKT - 1))
                mm.then_inc(sq_cons.sem, 1)
            p.emit("tensor", fmm)
            sq_cons.inc()
        smv = sq_cons.v

        def ffin_v(v, smv=smv):
            v.wait_ge(sq_cons.sem, smv)
            v.tensor_single_scalar(out=mu_row[:], in_=b.bank(bA)[0:1, :],
                                   scalar=1.0 / D, op=ALU.mult)
            v.tensor_single_scalar(out=rstd_row[:], in_=b.bank(bB)[0:1, :],
                                   scalar=1.0 / D, op=ALU.mult)
            v.drain()
            v.tensor_tensor(out=tmp_row[:], in0=mu_row[:], in1=mu_row[:], op=ALU.mult)
            v.drain()
            v.tensor_tensor(out=rstd_row[:], in0=rstd_row[:], in1=tmp_row[:],
                            op=ALU.subtract)
            v.drain()
            v.tensor_single_scalar(out=rstd_row[:], in_=rstd_row[:], scalar=EPS_LN,
                                   op=ALU.add).then_inc(stat_fin.sem, 1)
        p.emit("vector", ffin_v)
        sfv = stat_fin.inc()

        def ffin_s(s, sfv=sfv):
            s.wait_ge(stat_fin.sem, sfv)
            s.activation(out=rstd_row[:], in_=rstd_row[:], func=AF.Ln)
            s.drain()
            s.activation(out=rstd_row[:], in_=rstd_row[:], func=AF.Exp,
                         scale=-0.5).then_inc(stat_fin.sem, 1)
        p.emit("scalar", ffin_s)
        sfv2 = stat_fin.inc()

        def fbc(t, sfv2=sfv2):
            t.wait_ge(stat_fin.sem, sfv2)
            t.matmul(b.bank(bA), b.ones_row_f[:], mu_row[:], start=True, stop=True)
            t.matmul(b.bank(bB), b.ones_row_f[:], rstd_row[:],
                     start=True, stop=True).then_inc(bc_done.sem, 1)
        p.emit("tensor", fbc)
        bcv = bc_done.inc()

        for kt in range(NKT):
            gi = tt2 * NKT + kt
            buf = xa_f[:, (gi % 2) * 512:(gi % 2) * 512 + 512]

            def fap(v, kt=kt, tt2=tt2, gi=gi, buf=buf, bcv=bcv):
                if kt == 0:
                    v.wait_ge(bc_done.sem, bcv)
                if gi >= 2:
                    v.wait_ge(st.sem, (gi - 1) * 16)
                v.tensor_tensor(out=buf, in0=r_bf[:, rcol(kt, tt2):rcol(kt, tt2) + 512],
                                in1=b.bank(bA), op=ALU.subtract)
                v.drain()
                v.tensor_tensor(out=buf, in0=buf, in1=b.bank(bB),
                                op=ALU.mult).then_inc(ap_done.sem, 1)
            p.emit("vector", fap, reads=[r_bf[:]], writes=[xa_f[:]])
            apv = ap_done.inc()

            def fst(s, kt=kt, tt2=tt2, buf=buf, apv=apv):
                s.wait_ge(ap_done.sem, apv)
                s.dma_start(out=out_hbm.ap()[kt * 128:(kt + 1) * 128,
                                             tt2 * 512:tt2 * 512 + 512],
                            in_=buf).then_inc(st.sem, 16)
            p.emit("sync", fst)
            st.inc(16)

    class PL:
        pass
    pl = PL()
    pl.slice = emit_slice
    pl.sems = {"ap_done": ap_done, "sq_cons": sq_cons, "sq_done": sq_done}
    return pl


def emit_postln_store(b, r_bf, out_hbm, st, r_ready, rval_fn, stat_banks=(2, 3),
                      pref="", rcol=None, bufs=None):
    pl = make_postln(b, r_bf, out_hbm, st, r_ready, rval_fn, stat_banks, pref,
                     rcol, bufs)
    for tt2 in range(NS):
        pl.slice(tt2)
    return pl.sems


def emit_phase_barrier(b, st):
    p = b.p
    if b.phase == b.n_phases - 1:
        def ffinal(s, target=st.v):
            s.wait_ge(st.sem, target)
        p.emit("sync", ffinal)
        return
    ph, pdone = b.ph, b.pdone
    bsc = b.sb("bscratch", [1, 2], F32)

    def fsy(s, stv=st.v):
        s.wait_ge(st.sem, stv)
        s.dma_start(out=bsc[0:1, 0:1], in_=bsc[0:1, 1:2]).then_inc(pdone.sem, 16)
    p.emit("sync", fsy)
    pdone.inc(16)

    def ftn(t):
        t.nop().then_inc(pdone.sem, 1)
    p.emit("tensor", ftn)
    pdone.inc(1)

    def fsn(s):
        s.drain()
        s.nop().then_inc(pdone.sem, 1)
    p.emit("scalar", fsn)
    pdone.inc(1)

    tgt = pdone.inc(1)

    def fvn(v, tgt=tgt, phv=b.phase + 1):
        v.drain()
        v.nop().then_inc(pdone.sem, 1)
        v.wait_ge(pdone.sem, tgt)
        v.nop().then_inc(ph.sem, 1)
    p.emit("vector", fvn)
    b.ph.inc(1)


def build_l0ffn(b=None):
    standalone = b is None
    if standalone:
        b = Build("l0ffn")
    p = b.p
    x_in = b.dram_in("x_in", [D, T])
    u1 = b.dram_in("U1", [D, 256])
    v1 = b.dram_in("V1", [256, D * 4])
    b1 = b.dram_in("b1", [D * 4, 1])
    u2 = b.dram_in("U2", [D * 4, 256])
    v2 = b.dram_in("V2", [256, D])
    b2 = b.dram_in("b2", [D, 1])
    x_out = b.dram_out("x_out", [D, T])

    with (b.stack if standalone else contextlib.nullcontext()):
        if standalone:
            b.psum_arena()
            common_consts(b)
        ldc = b.ldc
        x_bf = b.sb("x_bf", [128, NKT * T], BF16)
        r_bf = b.sb("r_bf", [128, NKT * T], BF16)
        u1_s = b.sb("u1_s", [128, NKT * 256], BF16)
        v1_s = b.sb("v1_s", [128, 2 * D * 4], BF16)
        u2_s = b.sb("u2_s", [128, 32 * 256], BF16)
        v2_s = b.sb("v2_s", [128, 2 * D], BF16)
        b1_s = b.sb("b1_s", [128, 32], F32)
        b2_s = b.sb("b2_s", [128, 8], F32)
        p_tm = b.sb("p_tm", [128, 2 * 256], BF16)
        p_rm = b.sb("p_rm", [128, 2 * T], BF16)
        h_bf = b.sb("h_bf", [128, 2 * 512], BF16)
        q_cm = b.sb("q_cm", [128, 2 * 512], BF16)

        ld = ldc
        mmP = b.sem("mmP")
        evP = b.sem("evP")
        trP = b.sem("trP")
        evT = b.sem("evT")
        mmH = b.sem("mmH")
        evH = b.sem("evH")
        mmQ = b.sem("mmQ")
        evQ = b.sem("evQ")
        mmO = b.sem("mmO")
        evO = b.sem("evO")
        st = b.sem("st")

        def floads(g, _ph=b.ph, _pn=b.phase, _fu=b.fused):
            if _fu and _pn > 0:
                g.wait_ge(_ph.sem, _pn)
            for kt in range(NKT):
                g.dma_start(out=x_bf[:, kt * T:(kt + 1) * T],
                            in_=x_in.ap()[kt * 128:(kt + 1) * 128, :]).then_inc(ld.sem, 16)
            for kt in range(NKT):
                g.dma_start(out=u1_s[:, kt * 256:(kt + 1) * 256],
                            in_=u1.ap()[kt * 128:(kt + 1) * 128, :]).then_inc(ld.sem, 16)
            for kt in range(2):
                g.dma_start(out=v1_s[:, kt * (D * 4):(kt + 1) * (D * 4)],
                            in_=v1.ap()[kt * 128:(kt + 1) * 128, :]).then_inc(ld.sem, 16)
            for mt in range(32):
                g.dma_start(out=u2_s[:, mt * 256:(mt + 1) * 256],
                            in_=u2.ap()[mt * 128:(mt + 1) * 128, :]).then_inc(ld.sem, 16)
            for kt in range(2):
                g.dma_start(out=v2_s[:, kt * D:(kt + 1) * D],
                            in_=v2.ap()[kt * 128:(kt + 1) * 128, :]).then_inc(ld.sem, 16)
            g.dma_start(out=b1_s[:].unsqueeze(2),
                        in_=b1.ap().rearrange("(a p) o -> p a o", p=128),
                        allow_slow_non_contiguous=True).then_inc(ld.sem, 16)
            g.dma_start(out=b2_s[:].unsqueeze(2),
                        in_=b2.ap().rearrange("(a p) o -> p a o", p=128),
                        allow_slow_non_contiguous=True).then_inc(ld.sem, 16)
        p.emit("gpsimd", floads)
        ld.inc(16 * (NKT + NKT + 2 + 32 + 2 + 2))
        LDALL = ld.v

        for tt in range(NTT):
            def fmmP(t, tt=tt):
                if tt == 0:
                    t.wait_ge(ld.sem, LDALL)
                    t.wait_ge(ldc.sem, 64)
                if tt >= 1:
                    t.wait_ge(evP.sem, tt)
                for kt in range(NKT):
                    mm = t.matmul(b.bank(0, 256),
                                  x_bf[:, kt * T + tt * 128: kt * T + tt * 128 + 128],
                                  u1_s[:, kt * 256:(kt + 1) * 256],
                                  start=(kt == 0), stop=(kt == NKT - 1))
                mm.then_inc(mmP.sem, 1)
            p.emit("tensor", fmmP)
            mv = mmP.inc()

            def fevP(v, tt=tt, mv=mv):
                v.wait_ge(mmP.sem, mv)
                if tt >= 2:
                    v.wait_ge(trP.sem, 2 * (tt - 1))
                v.tensor_copy(out=p_tm[:, (tt % 2) * 256:(tt % 2) * 256 + 256],
                              in_=b.bank(0, 256)).then_inc(evP.sem, 1)
            p.emit("vector", fevP, writes=[p_tm[:]])
            evP.inc()

            for rt in range(2):
                idx = tt * 2 + rt

                def ftr(t, tt=tt, rt=rt, idx=idx):
                    t.wait_ge(evP.sem, tt + 1)
                    if idx >= 2:
                        t.wait_ge(evT.sem, idx - 1)
                    t.transpose(b.bank_bf(1 + idx % 2, 128),
                                p_tm[:, (tt % 2) * 256 + rt * 128:
                                     (tt % 2) * 256 + rt * 128 + 128],
                                b.ident[:]).then_inc(trP.sem, 1)
                p.emit("tensor", ftr)
                tv = trP.inc()

                def fev(v, tt=tt, rt=rt, idx=idx, tv=tv):
                    v.wait_ge(trP.sem, tv)
                    v.tensor_copy(out=p_rm[:, rt * T + tt * 128: rt * T + tt * 128 + 128],
                                  in_=b.bank_bf(1 + idx % 2, 128)).then_inc(evT.sem, 1)
                p.emit("vector", fev, writes=[p_rm[:]])
                evT.inc()

        for tt2 in range(NS):
            for mt in range(32):
                gi = tt2 * 32 + mt

                def fmmH(t, tt2=tt2, mt=mt, gi=gi):
                    if gi == 0:
                        t.wait_ge(evT.sem, 2 * NTT)
                    if gi >= 2:
                        t.wait_ge(evH.sem, gi - 1)
                    for kt in range(2):
                        mm = t.matmul(b.bank(4 + gi % 2),
                                      v1_s[:, kt * (D * 4) + mt * 128:
                                           kt * (D * 4) + mt * 128 + 128],
                                      p_rm[:, kt * T + tt2 * 512: kt * T + tt2 * 512 + 512],
                                      start=(kt == 0), stop=(kt == 1))
                    mm.then_inc(mmH.sem, 1)
                p.emit("tensor", fmmH)
                hv = mmH.inc()

                def fevH(s, mt=mt, gi=gi, hv=hv):
                    s.wait_ge(mmH.sem, hv)
                    s.activation(out=h_bf[:, (gi % 2) * 512:(gi % 2) * 512 + 512],
                                 in_=b.bank(4 + gi % 2), func=AF.Gelu,
                                 bias=b1_s[:, mt:mt + 1]).then_inc(evH.sem, 1)
                p.emit("scalar", fevH, writes=[h_bf[:]])
                evH.inc()

                def fmmQ(t, mt=mt, gi=gi, tt2=tt2):
                    t.wait_ge(evH.sem, gi + 1)
                    if mt == 0 and tt2 > 0:
                        t.wait_ge(evQ.sem, tt2)
                    for mh in range(2):
                        mm = t.matmul(b.bank(6 + mh),
                                      u2_s[:, mt * 256 + mh * 128:
                                           mt * 256 + mh * 128 + 128],
                                      h_bf[:, (gi % 2) * 512:(gi % 2) * 512 + 512],
                                      start=(mt == 0), stop=(mt == 31))
                    if mt == 31:
                        mm.then_inc(mmQ.sem, 1)
                p.emit("tensor", fmmQ)
            qv = mmQ.inc()

            def fevQ(v, qv=qv):
                v.wait_ge(mmQ.sem, qv)
                v.tensor_copy(out=q_cm[:, 0:512], in_=b.bank(6))
                v.tensor_copy(out=q_cm[:, 512:1024], in_=b.bank(7)).then_inc(evQ.sem, 1)
            p.emit("vector", fevQ, writes=[q_cm[:]])
            eqv = evQ.inc()

            for oc in range(NKT):
                gio = tt2 * NKT + oc

                def fmmO(t, oc=oc, gio=gio, eqv=eqv):
                    if oc == 0:
                        t.wait_ge(evQ.sem, eqv)
                    if gio >= 2:
                        t.wait_ge(evO.sem, gio - 1)
                    for kt in range(2):
                        mm = t.matmul(b.bank(gio % 2),
                                      v2_s[:, kt * D + oc * 128: kt * D + oc * 128 + 128],
                                      q_cm[:, kt * 512:(kt + 1) * 512],
                                      start=(kt == 0), stop=(kt == 1))
                    mm.then_inc(mmO.sem, 1)
                p.emit("tensor", fmmO)
                ov = mmO.inc()

                def fevO(v, oc=oc, tt2=tt2, gio=gio, ov=ov):
                    v.wait_ge(mmO.sem, ov)
                    v.scalar_tensor_tensor(
                        out=r_bf[:, oc * T + tt2 * 512: oc * T + tt2 * 512 + 512],
                        in0=x_bf[:, oc * T + tt2 * 512: oc * T + tt2 * 512 + 512],
                        scalar=b2_s[:, oc:oc + 1],
                        in1=b.bank(gio % 2),
                        op0=ALU.add, op1=ALU.add).then_inc(evO.sem, 1)
                p.emit("vector", fevO, reads=[x_bf[:]], writes=[r_bf[:]])
                evO.inc()

        emit_postln_store(b, r_bf, x_out, st, evO, lambda tt2: (tt2 + 1) * NKT)

        if standalone:
            def ffinal(s, target=st.v):
                s.wait_ge(st.sem, target)
            p.emit("sync", ffinal)
            p.finalize()
        else:
            emit_phase_barrier(b, st)
    return b.nc


def stage_l0ffn(inputs):
    return {
        "U1": np.ascontiguousarray(np.asarray(inputs["L0_ffn_U1"], np.float32)),
        "V1": np.ascontiguousarray(np.asarray(inputs["L0_ffn_V1"], np.float32)),
        "b1": np.asarray(inputs["L0_ffn_b1"], np.float32).reshape(-1, 1).copy(),
        "U2": np.ascontiguousarray(np.asarray(inputs["L0_ffn_U2"], np.float32)),
        "V2": np.ascontiguousarray(np.asarray(inputs["L0_ffn_V2"], np.float32)),
        "b2": np.asarray(inputs["L0_ffn_b2"], np.float32).reshape(-1, 1).copy(),
        "consts": host_consts(),
    }


def build_fused():
    b = Build("fused")
    b.fused = True
    b.n_phases = 4
    x_in = b.nc.dram_tensor("x_in", [D, T], F32, kind="ExternalInput")
    h1 = b.nc.dram_tensor("h1", [D, T], F32, kind="Internal")
    h2 = b.nc.dram_tensor("h2", [D, T], F32, kind="Internal")
    h3 = b.nc.dram_tensor("h3", [D, T], F32, kind="Internal")
    out = b.nc.dram_tensor("out_cm", [D, T], F32, kind="ExternalOutput")
    with b.stack:
        b.psum_arena()
        common_consts(b)
        b.ph = b.sem("ph")
        b.pdone = b.sem("pdone")
        phases = [
            (build_l0attn, {"x_in": x_in, "x_out": h1}),
            (build_l0ffn, {"x_in": h1, "x_out": h2}),
            (build_l1attn, {"x_cm": h2, "x_out": h3}),
            (build_l1ffn, {"x_cm": h3, "x_out": out}),
        ]
        for k, (fn, io) in enumerate(phases):
            b.phase = k
            b.pref = "p%d_" % k
            b.io_map = io
            b.pstack = contextlib.ExitStack()
            fn(b=b)
            b.pstack.close()
            b.pstack = None
        b.pref = ""
        b.io_map = {}
        b.p.finalize()
    return b.nc


def kernel(**inputs):
    from concourse.bass_utils import run_bass_kernel_spmd
    x = np.asarray(inputs["L0_attn_x"], np.float32)
    nc = build_fused()
    base = {}
    for pref, stage_fn in (("p0_", stage_l0attn), ("p1_", stage_l0ffn),
                           ("p2_", stage_l1attn), ("p3_", stage_l1ffn)):
        for k, v in stage_fn(inputs).items():
            base["consts" if k == "consts" else pref + k] = v
    maps = []
    for bi in range(8):
        m = dict(base)
        m["x_in"] = np.ascontiguousarray(x[bi].T)
        maps.append(m)
    res = run_bass_kernel_spmd(nc, maps, core_ids=list(range(8)))
    return np.stack([np.asarray(res.results[i]["out_cm"], np.float32).T
                     for i in range(8)]).astype(np.float32)


def build_l0attn(stop_after=None, dump_z2=False, dump_r=False, canary=False,
                 b=None):
    p3 = stop_after in ('pass3', 'p3a', 'p3b')
    standalone = b is None
    if standalone:
        b = Build("l0attn")
    p = b.p
    x_in = b.dram_in("x_in", [D, T])
    winA = b.dram_in("winA", [D, 3 * D])
    bin_r = b.dram_in("bin_r", [1, 3 * D])
    hk1 = b.dram_in("hk1", [D, 384])
    hk2 = b.dram_in("hk2", [D, 384])
    gamv = b.dram_in("gamv", [D, 1])
    betv = b.dram_in("betv", [D, 1])
    wout = b.dram_in("wout", [D, D])
    bout = b.dram_in("bout", [D, 1])
    x_out = b.dram_out("x_out", [D, T])
    dbg = b.dram_out("dbg", [128, NKT * T]) if (stop_after or (dump_z2 and dump_z2 > 0)) else None
    dbg2 = b.nc.dram_tensor("dbg2", [128, NS * NKT * 512], BF16,
                            kind="ExternalOutput") if dump_r else None
    if canary:
        b.nc.dram_tensor("canary", [128, NKT * T], F32, kind="ExternalOutput")

    GSZ = 4
    NG = D // GSZ

    with (b.stack if standalone else contextlib.nullcontext()):
        if standalone:
            b.psum_arena()
            common_consts(b)
        ldc = b.ldc
        xn = b.sb("xn", [128, NKT * T], BF16)
        F = b.sb("F", [128, D * NTT], BF16)
        hank = b.sb("hank", [128, 2 * GSZ * 257], BF16)
        w_sl = b.sb("w_sl", [128, NKT * 1024], BF16)
        wout_s = b.sb("wout_s", [128, NKT * D], BF16)
        bin_s = b.sb("bin_s", [1, D], BF16)
        gam_s = b.sb("gam_s", [128, NKT], F32)
        bet_s = b.sb("bet_s", [128, NKT], F32)
        bout_s = b.sb("bout_s", [128, NKT], F32)
        xst = b.sb("xst", [128, 2 * NKT * 512], BF16)
        mu_full = b.sb("mu_full", [1, 512], F32)
        rs_full = b.sb("rs_full", [1, 512], F32)
        sqb = b.sb("sqb", [128, 2 * 512], BF16)
        t1f = b.sb("t1f", [128, 2 * 512], F32)
        xnr = b.sb("xnr", [128, 2 * NKT * 128], BF16)
        z2q = b.sb("z2q", [128, 2 * D], BF16)
        z2cm = xnr
        rsl = xst
        xres = b.sb("xres", [128, 2 * NKT * 128], BF16)

        ld = ldc
        ldw = ldc
        ldx = ldc
        ldr = ldc
        hkr = ldc
        sqd = b.sem("sqd")
        sqc = b.sem("sqc")
        stf = b.sem("stf")
        bcd = b.sem("bcd")
        apd = b.sem("apd")
        pj = b.sem("pj")
        pe = b.sem("pe")
        cv = b.sem("cv")
        ce = b.sem("ce")
        rv = b.sem("rv")
        tr = b.sem("tr")
        te = b.sem("te")
        wo = b.sem("wo")
        we = b.sem("we")
        st = b.sem("st")
        std_ = b.sem("std_")

        def floads(g, _ph=b.ph, _pn=b.phase, _fu=b.fused):
            if _fu and _pn > 0:
                g.wait_ge(_ph.sem, _pn)
            g.dma_start(out=gam_s[:].unsqueeze(2),
                        in_=gamv.ap().rearrange("(a p) o -> p a o", p=128),
                        allow_slow_non_contiguous=True).then_inc(ld.sem, 16)
            g.dma_start(out=bet_s[:].unsqueeze(2),
                        in_=betv.ap().rearrange("(a p) o -> p a o", p=128),
                        allow_slow_non_contiguous=True).then_inc(ld.sem, 16)
            g.dma_start(out=bout_s[:].unsqueeze(2),
                        in_=bout.ap().rearrange("(a p) o -> p a o", p=128),
                        allow_slow_non_contiguous=True).then_inc(ld.sem, 16)
            for kt in range(NKT):
                g.dma_start(out=wout_s[:, kt * D:(kt + 1) * D],
                            in_=wout.ap()[kt * 128:(kt + 1) * 128, :]).then_inc(ld.sem, 16)
        p.emit("gpsimd", floads)
        ld.inc(16 * (3 + NKT))
        LDALL = ld.v

        for tt2 in range(NS):
            slot = tt2 % 2
            for kt in range(NKT):
                def flx(g, kt=kt, tt2=tt2, slot=slot):
                    if kt == 0 and tt2 >= 2:
                        g.wait_ge(apd.sem, (tt2 - 1) * 2 * NKT)
                    g.dma_start(out=xst[:, slot * NKT * 512 + kt * 512:
                                        slot * NKT * 512 + kt * 512 + 512],
                                in_=x_in.ap()[kt * 128:(kt + 1) * 128,
                                              tt2 * 512:tt2 * 512 + 512]).then_inc(ldx.sem, 16)
                p.emit("gpsimd", flx)
                ldx.inc(16)
            lxv = ldx.v
            for kt in range(NKT):
                gi = tt2 * NKT + kt

                def fsq(s, kt=kt, tt2=tt2, slot=slot, gi=gi, lxv=lxv):
                    if kt == 0:
                        s.wait_ge(ldx.sem, lxv)
                    if gi >= 2:
                        s.wait_ge(sqc.sem, gi - 1)
                    s.activation(out=sqb[:, (gi % 2) * 512:(gi % 2) * 512 + 512],
                                 in_=xst[:, slot * NKT * 512 + kt * 512:
                                         slot * NKT * 512 + kt * 512 + 512],
                                 func=AF.Square).then_inc(sqd.sem, 1)
                p.emit("scalar", fsq, reads=[xst[:]], writes=[sqb[:]])
                sv = sqd.inc()

                def fmm(t, kt=kt, tt2=tt2, slot=slot, gi=gi, sv=sv, lxv=lxv):
                    if kt == 0:
                        t.wait_ge(ldx.sem, lxv)
                        t.wait_ge(ldc.sem, 64)
                        if tt2 > 0:
                            t.wait_ge(apd.sem, tt2 * 2 * NKT)
                    t.wait_ge(sqd.sem, sv)
                    t.matmul(b.bank(2)[0:1, :], b.ones_col[:],
                             xst[:, slot * NKT * 512 + kt * 512:
                                 slot * NKT * 512 + kt * 512 + 512],
                             start=(kt == 0), stop=(kt == NKT - 1))
                    t.matmul(b.bank(3)[0:1, :], b.ones_col[:],
                             sqb[:, (gi % 2) * 512:(gi % 2) * 512 + 512],
                             start=(kt == 0), stop=(kt == NKT - 1)).then_inc(sqc.sem, 1)
                p.emit("tensor", fmm)
                sqc.inc()
            scv = sqc.v

            def ffin_v(v, tt2=tt2, scv=scv):
                v.wait_ge(sqc.sem, scv)
                v.tensor_single_scalar(out=mu_full[0:1, :],
                                       in_=b.bank(2)[0:1, :], scalar=1.0 / D, op=ALU.mult)
                v.tensor_single_scalar(out=rs_full[0:1, :],
                                       in_=b.bank(3)[0:1, :], scalar=1.0 / D, op=ALU.mult)
                v.drain()
                v.tensor_tensor(out=t1f[0:1, 0:512],
                                in0=mu_full[0:1, :],
                                in1=mu_full[0:1, :], op=ALU.mult)
                v.drain()
                v.tensor_tensor(out=rs_full[0:1, :],
                                in0=rs_full[0:1, :],
                                in1=t1f[0:1, 0:512], op=ALU.subtract)
                v.drain()
                v.tensor_single_scalar(out=rs_full[0:1, :],
                                       in_=rs_full[0:1, :],
                                       scalar=EPS_LN, op=ALU.add).then_inc(stf.sem, 1)
            p.emit("vector", ffin_v)
            s1 = stf.inc()

            def ffin_s(s, tt2=tt2, s1=s1):
                s.wait_ge(stf.sem, s1)
                s.activation(out=rs_full[0:1, :],
                             in_=rs_full[0:1, :], func=AF.Ln)
                s.drain()
                s.activation(out=rs_full[0:1, :],
                             in_=rs_full[0:1, :], func=AF.Exp,
                             scale=-0.5).then_inc(stf.sem, 1)
            p.emit("scalar", ffin_s)
            s2 = stf.inc()

            def fbc(t, tt2=tt2, s2=s2):
                t.wait_ge(stf.sem, s2)
                t.matmul(b.bank(2), b.ones_row_f[:],
                         mu_full[0:1, :], start=True, stop=True)
                t.matmul(b.bank(3), b.ones_row_f[:],
                         rs_full[0:1, :],
                         start=True, stop=True).then_inc(bcd.sem, 1)
            p.emit("tensor", fbc)
            bv = bcd.inc()

            for kt in range(NKT):
                def fap(v, kt=kt, tt2=tt2, slot=slot, bv=bv):
                    if kt == 0:
                        v.wait_ge(bcd.sem, bv)
                    if kt >= 2:
                        v.wait_ge(apd.sem, tt2 * 2 * NKT + 2 * (kt - 1))
                    v.tensor_tensor(out=t1f[:, (kt % 2) * 512:(kt % 2) * 512 + 512],
                                    in0=xst[:, slot * NKT * 512 + kt * 512:
                                            slot * NKT * 512 + kt * 512 + 512],
                                    in1=b.bank(2), op=ALU.subtract)
                    v.drain()
                    v.tensor_tensor(out=t1f[:, (kt % 2) * 512:(kt % 2) * 512 + 512],
                                    in0=t1f[:, (kt % 2) * 512:(kt % 2) * 512 + 512],
                                    in1=b.bank(3), op=ALU.mult).then_inc(apd.sem, 1)
                p.emit("vector", fap, reads=[xst[:]], writes=[t1f[:]])
                av = apd.inc()

                def fapg(v, kt=kt, tt2=tt2, av=av):
                    v.wait_ge(apd.sem, av)
                    v.tensor_scalar(out=xn[:, kt * T + tt2 * 512: kt * T + tt2 * 512 + 512],
                                    in0=t1f[:, (kt % 2) * 512:(kt % 2) * 512 + 512],
                                    scalar1=gam_s[:, kt:kt + 1],
                                    scalar2=bet_s[:, kt:kt + 1],
                                    op0=ALU.mult, op1=ALU.add).then_inc(apd.sem, 1)
                p.emit("vector", fapg, reads=[t1f[:]], writes=[xn[:]])
                apd.inc()
        APD_ALL = apd.v

        def dump_and_stop(src_bf, ready_sem, ready_val):
            base = st.v

            def fd(g, rv=ready_val, base=base):
                g.wait_ge(ready_sem.sem, rv)
                for kt in range(NKT):
                    g.dma_start(out=dbg.ap()[:, kt * T:(kt + 1) * T],
                                in_=src_bf[:, kt * T:(kt + 1) * T]).then_inc(st.sem, 16)
                g.wait_ge(st.sem, base + 16 * NKT)
            p.emit("gpsimd", fd)
            st.inc(16 * NKT)
            p.finalize()

        if stop_after == 'p0':
            dump_and_stop(xn, apd, apd.v)
            return b.nc

        def proj_pass(pass_i, reversed_lhs, gate_from_F, out_mode):
            base = pj.v
            ce_target = ce.v
            need = pe.v

            def flw(g, pass_i=pass_i, need=need):
                if pass_i > 0:
                    g.wait_ge(pe.sem, need)
                for kt in range(NKT):
                    g.dma_start(out=w_sl[:, kt * 1024:(kt + 1) * 1024],
                                in_=winA.ap()[kt * 128:(kt + 1) * 128,
                                              pass_i * 1024:(pass_i + 1) * 1024]).then_inc(ldw.sem, 16)
                g.dma_start(out=bin_s[:],
                            in_=bin_r.ap()[:, pass_i * 1024:(pass_i + 1) * 1024]).then_inc(ldw.sem, 16)
            p.emit("gpsimd", flw)
            ldw.inc(16 * (NKT + 1))
            lwv = ldw.v

            for tt in range(NTT):
                if reversed_lhs:
                    for kt in range(NKT):
                        def frv(v, kt=kt, tt=tt):
                            if kt == 0 and tt >= 2:
                                v.wait_ge(pj.sem, base + (tt - 1) * 2)
                            src = bass_rust.AP(
                                tensor=xn[:].tensor,
                                offset=kt * T + tt * 128 + 127,
                                ap=[[NKT * T, 128], [-1, 128]])
                            v.tensor_copy(
                                out=xnr[:, (tt % 2) * NKT * 128 + kt * 128:
                                        (tt % 2) * NKT * 128 + kt * 128 + 128],
                                in_=src).then_inc(rv.sem, 1)
                        p.emit("vector", frv, reads=[xn[:]], writes=[xnr[:]])
                        rv.inc()
                rvv = rv.v

                for ns in range(2):
                    gi = tt * 2 + ns

                    def fpj(t, tt=tt, ns=ns, gi=gi, lwv=lwv, rvv=rvv,
                            pass_i=pass_i, reversed_lhs=reversed_lhs):
                        if gi == 0:
                            t.wait_ge(ldw.sem, lwv)
                            if pass_i == 0:
                                t.wait_ge(apd.sem, APD_ALL)
                        if reversed_lhs:
                            t.wait_ge(rv.sem, rvv)
                        if gi >= 2:
                            t.wait_ge(pe.sem, base + gi - 1)
                        for kt in range(NKT):
                            if reversed_lhs:
                                lhs = xnr[:, (tt % 2) * NKT * 128 + kt * 128:
                                          (tt % 2) * NKT * 128 + kt * 128 + 128]
                            else:
                                lhs = xn[:, kt * T + tt * 128: kt * T + tt * 128 + 128]
                            t.matmul(b.bank(gi % 2), lhs,
                                     w_sl[:, kt * 1024 + ns * 512: kt * 1024 + ns * 512 + 512],
                                     start=(kt == 0), stop=False)
                        mm = t.matmul(b.bank(gi % 2), b.ones_row[:],
                                      bin_s[:, ns * 512: ns * 512 + 512],
                                      start=False, stop=True)
                        mm.then_inc(pj.sem, 1)
                    p.emit("tensor", fpj)
                    pjv = pj.inc()

                    def fpe(v, tt=tt, ns=ns, gi=gi, pjv=pjv, out_mode=out_mode,
                            ce_target=ce_target):
                        v.wait_ge(pj.sem, pjv)
                        if out_mode != 'fold' and gi == 0:
                            v.wait_ge(ce.sem, ce_target)
                        fcols = bass_rust.AP(
                            tensor=F[:].tensor,
                            offset=(ns * 512) * 32 + tt,
                            ap=[[D * NTT, 128], [32, 512], [1, 1]])
                        if out_mode == 'fold':
                            v.tensor_copy(out=fcols, in_=b.bank(gi % 2)).then_inc(pe.sem, 1)
                        elif out_mode == 'gate_fold':
                            v.tensor_tensor(out=fcols, in0=b.bank(gi % 2), in1=fcols,
                                            op=ALU.mult).then_inc(pe.sem, 1)
                        else:
                            v.tensor_tensor(out=z2q[:, (tt % 2) * D + ns * 512:
                                                    (tt % 2) * D + ns * 512 + 512],
                                            in0=b.bank(gi % 2), in1=fcols,
                                            op=ALU.mult).then_inc(pe.sem, 1)
                    p.emit("vector", fpe, reads=[F[:]], writes=[F[:], z2q[:]])
                    pe.inc()
            return base

        def conv_pass(hk_dram):
            cbase = cv.v
            ebase = ce.v
            hbase = hkr.v
            pe_target = pe.v
            for g in range(NG):
                def fhk(gq, g=g, hbase=hbase):
                    if g >= 2:
                        gq.wait_ge(cv.sem, cbase + g - 1)
                    elif cbase > 0:
                        gq.wait_ge(cv.sem, cbase)
                    src = bass_rust.AP(tensor=hk_dram, offset=g * GSZ * 384,
                                       ap=[[1, 128], [384, GSZ], [1, 256]])
                    dst = bass_rust.AP(tensor=hank[:].tensor,
                                       offset=(g % 2) * GSZ * 257,
                                       ap=[[2 * GSZ * 257, 128], [257, GSZ], [1, 256]])
                    gq.dma_start(out=dst, in_=src,
                                 allow_slow_non_contiguous=True).then_inc(hkr.sem, 16)
                p.emit("gpsimd", fhk)
                hkr.inc(16)
                hv = hkr.v

                def fcv(t, g=g, hv=hv, pe_target=pe_target):
                    t.wait_ge(hkr.sem, hv)
                    if g == 0:
                        t.wait_ge(pe.sem, pe_target)
                    if g >= 2:
                        t.wait_ge(ce.sem, ebase + g - 1)
                    for cl in range(GSZ):
                        cg = g * GSZ + cl
                        hb = (g % 2) * GSZ * 257 + cl * 257
                        t.matmul(b.bank(4 + g % 2)[:, cl * 32:(cl + 1) * 32],
                                 hank[:, hb + 128: hb + 256],
                                 F[:, cg * 32: cg * 32 + 32],
                                 start=True, stop=False)
                        mm = t.matmul(b.bank(4 + g % 2)[:, cl * 32 + 1:(cl + 1) * 32],
                                      hank[:, hb: hb + 128],
                                      F[:, cg * 32: cg * 32 + 31],
                                      start=False, stop=True)
                    mm.then_inc(cv.sem, 1)
                p.emit("tensor", fcv)
                cvv = cv.inc()

                def fce(v, g=g, cvv=cvv):
                    v.wait_ge(cv.sem, cvv)
                    v.tensor_copy(out=F[:, g * GSZ * 32:(g + 1) * GSZ * 32],
                                  in_=b.bank(4 + g % 2, GSZ * 32)).then_inc(ce.sem, 1)
                p.emit("vector", fce, reads=[F[:]], writes=[F[:]])
                ce.inc()

        proj_pass(0, False, False, 'fold')
        if stop_after == 'pass1':
            dump_and_stop(F, pe, pe.v)
            return b.nc
        conv_pass(hk1)
        if stop_after == 'conv1':
            dump_and_stop(F, ce, ce.v)
            return b.nc
        proj_pass(1, True, True, 'gate_fold')
        if stop_after == 'pass2':
            dump_and_stop(F, pe, pe.v)
            return b.nc
        conv_pass(hk2)
        if stop_after == 'conv2':
            dump_and_stop(F, ce, ce.v)
            return b.nc

        base3 = pj.v

        CE_ALL = ce.v

        def flw3(g, ce_all=None):
            g.wait_ge(ce.sem, CE_ALL)
            for kt in range(NKT):
                g.dma_start(out=w_sl[:, kt * 1024:(kt + 1) * 1024],
                            in_=winA.ap()[kt * 128:(kt + 1) * 128,
                                          2 * 1024:3 * 1024]).then_inc(ldw.sem, 16)
            g.dma_start(out=bin_s[:],
                        in_=bin_r.ap()[:, 2 * 1024:3 * 1024]).then_inc(ldw.sem, 16)
        p.emit("gpsimd", flw3)
        ldw.inc(16 * (NKT + 1))
        lw3 = ldw.v

        pl = make_postln(
            b, rsl, x_out, st, we, lambda tt2: (tt2 + 1) * 4 * NKT,
            stat_banks=(2, 3), pref="o",
            rcol=lambda kt, tt2: (tt2 % 2) * NKT * 512 + kt * 512,
            bufs={"sq": sqb, "xa_f": t1f, "mu_row": mu_full, "rstd_row": rs_full})
        pln = pl.sems
        for tt in range(NTT):
            tt2 = tt // 4
            for kt in range(NKT if not p3 else 0):
                def flr(g, kt=kt, tt=tt, pebase3=base3):
                    if kt == 0 and tt >= 2:
                        g.wait_ge(we.sem, (tt - 1) * NKT)
                        if dump_z2 in (3, -1):
                            g.wait_ge(pe.sem, pebase3 + 2 * tt)
                    g.dma_start(out=xres[:, (tt % 2) * NKT * 128 + kt * 128:
                                         (tt % 2) * NKT * 128 + kt * 128 + 128],
                                in_=x_in.ap()[kt * 128:(kt + 1) * 128,
                                              tt * 128:tt * 128 + 128]).then_inc(ldr.sem, 16)
                p.emit("gpsimd", flr)
                ldr.inc(16)
            lrv = ldr.v

            for ns in range(2):
                gi = tt * 2 + ns

                def fpj3(t, tt=tt, ns=ns, gi=gi):
                    if gi == 0:
                        t.wait_ge(ldw.sem, lw3)
                        t.wait_ge(ce.sem, CE_ALL)
                    if gi >= 2:
                        t.wait_ge(pe.sem, base3 + gi - 1)
                    for kt in range(NKT):
                        t.matmul(b.bank(gi % 2),
                                 xn[:, kt * T + tt * 128: kt * T + tt * 128 + 128],
                                 w_sl[:, kt * 1024 + ns * 512: kt * 1024 + ns * 512 + 512],
                                 start=(kt == 0), stop=False)
                    mm = t.matmul(b.bank(gi % 2), b.ones_row[:],
                                  bin_s[:, ns * 512: ns * 512 + 512],
                                  start=False, stop=True)
                    mm.then_inc(pj.sem, 1)
                p.emit("tensor", fpj3)
                pjv = pj.inc()

                def fpe3(v, tt=tt, ns=ns, gi=gi, pjv=pjv):
                    v.wait_ge(pj.sem, pjv)
                    if tt >= 2 and ns == 0:
                        if stop_after in ('pass3', 'p3a'):
                            v.wait_ge(st.sem, 16 * (tt - 1))
                        elif stop_after == 'p3b':
                            pass
                        else:
                            v.wait_ge(tr.sem, (tt - 1) * NKT)
                            if dump_z2 == 1:
                                v.wait_ge(std_.sem, 16 * (tt - 1))
                    fcols = bass_rust.AP(
                        tensor=F[:].tensor,
                        offset=(ns * 512) * 32 + tt,
                        ap=[[D * NTT, 128], [32, 512], [1, 1]])
                    if stop_after == 'p3a':
                        v.tensor_copy(out=z2q[:, (tt % 2) * D + ns * 512:
                                              (tt % 2) * D + ns * 512 + 512],
                                      in_=b.bank(gi % 2)).then_inc(pe.sem, 1)
                    else:
                        v.tensor_tensor(out=z2q[:, (tt % 2) * D + ns * 512:
                                                (tt % 2) * D + ns * 512 + 512],
                                        in0=b.bank(gi % 2),
                                        in1=fcols, op=ALU.mult).then_inc(pe.sem, 1)
                p.emit("vector", fpe3, reads=[F[:]], writes=[z2q[:]])
                pe.inc()
            pev = pe.v
            if dump_z2 and dump_z2 > 0:
                def fdz(g, tt=tt, pev=pev):
                    g.wait_ge(pe.sem, pev)
                    g.dma_start(out=dbg.ap()[:, tt * 1024:(tt + 1) * 1024],
                                in_=z2q[:, (tt % 2) * D:(tt % 2) * D + D]).then_inc(std_.sem, 16)
                p.emit("gpsimd", fdz)
                std_.inc(16)
            if stop_after in ('pass3', 'p3a'):
                def fd3(g, tt=tt, pev=pev):
                    g.wait_ge(pe.sem, pev)
                    g.dma_start(out=dbg.ap()[:, tt * 1024:(tt + 1) * 1024],
                                in_=z2q[:, (tt % 2) * D:(tt % 2) * D + D]).then_inc(st.sem, 16)
                p.emit("gpsimd", fd3)
                st.inc(16)
                continue
            if stop_after == 'p3b':
                if tt == NTT - 1:
                    def fd3b(g, tt=tt, pev=pev):
                        g.wait_ge(pe.sem, pev)
                        g.dma_start(out=dbg.ap()[:, 0:1024],
                                    in_=z2q[:, (tt % 2) * D:(tt % 2) * D + D]).then_inc(st.sem, 16)
                    p.emit("gpsimd", fd3b)
                    st.inc(16)
                continue

            for kt in range(NKT):
                def ftr(t, kt=kt, tt=tt, pev=pev):
                    if kt == 0:
                        t.wait_ge(pe.sem, pev)
                    idx = tt * NKT + kt
                    if idx >= 2:
                        t.wait_ge(te.sem, idx - 1)
                    t.transpose(b.bank_bf(4 + idx % 2, 128),
                                z2q[:, (tt % 2) * D + kt * 128:
                                    (tt % 2) * D + kt * 128 + 128],
                                b.ident[:]).then_inc(tr.sem, 1)
                p.emit("tensor", ftr)
                trv = tr.inc()

                def fte(v, kt=kt, tt=tt, trv=trv):
                    v.wait_ge(tr.sem, trv)
                    if kt == 0 and tt >= 2:
                        v.wait_ge(wo.sem, tt - 1)
                    idx = tt * NKT + kt
                    v.tensor_copy(out=z2cm[:, (tt % 2) * NKT * 128 + kt * 128:
                                          (tt % 2) * NKT * 128 + kt * 128 + 128],
                                  in_=b.bank_bf(4 + idx % 2, 128)).then_inc(te.sem, 1)
                p.emit("vector", fte, writes=[z2cm[:]])
                te.inc()
            tev = te.v

            def fwo(t, tt=tt, tev=tev, lrv=lrv):
                t.wait_ge(te.sem, tev)
                t.wait_ge(ldr.sem, lrv)
                if tt >= 2:
                    t.wait_ge(we.sem, (tt - 1) * NKT)
                for oc in range(NKT):
                    bank = 6 + oc // 4
                    coff = (oc % 4) * 128
                    for kt in range(NKT):
                        t.matmul(b.bank(bank)[:, coff:coff + 128],
                                 wout_s[:, kt * D + oc * 128: kt * D + oc * 128 + 128],
                                 z2cm[:, (tt % 2) * NKT * 128 + kt * 128:
                                      (tt % 2) * NKT * 128 + kt * 128 + 128],
                                 start=(kt == 0), stop=False)
                    mm = t.matmul(b.bank(bank)[:, coff:coff + 128],
                                  b.ident[:],
                                  xres[:, (tt % 2) * NKT * 128 + oc * 128:
                                       (tt % 2) * NKT * 128 + oc * 128 + 128],
                                  start=False, stop=True)
                mm.then_inc(wo.sem, 1)
            p.emit("tensor", fwo)
            wov = wo.inc()

            for oc in range(NKT):
                def fwe(v, oc=oc, tt=tt, tt2=tt2, wov=wov, lrv=lrv):
                    if oc == 0:
                        v.wait_ge(wo.sem, wov)
                        v.wait_ge(ldr.sem, lrv)
                        if tt % 4 == 0 and tt2 >= 2 and pln:
                            v.wait_ge(pln['ap_done'].sem, (tt2 - 1) * NKT)
                            v.wait_ge(pln['sq_cons'].sem, (tt2 - 1) * NKT)
                    bank = 6 + oc // 4
                    coff = (oc % 4) * 128
                    v.scalar_tensor_tensor(
                        out=rsl[:, (tt2 % 2) * NKT * 512 + oc * 512 + (tt % 4) * 128:
                                (tt2 % 2) * NKT * 512 + oc * 512 + (tt % 4) * 128 + 128],
                        in0=xres[:, (tt % 2) * NKT * 128 + oc * 128:
                                 (tt % 2) * NKT * 128 + oc * 128 + 128],
                        scalar=bout_s[:, oc:oc + 1],
                        in1=b.bank(bank)[:, coff:coff + 128],
                        op0=ALU.add, op1=ALU.add).then_inc(we.sem, 1)
                p.emit("vector", fwe, reads=[xres[:]], writes=[rsl[:]])
                we.inc()
            if tt % 4 == 3:
                if dump_r:
                    def fdr(s, tt=tt, wev=we.v):
                        s.wait_ge(we.sem, wev)
                        tt2 = tt // 4
                        s.dma_start(out=dbg2.ap()[:, tt2 * 4096:(tt2 + 1) * 4096],
                                    in_=rsl[:, (tt2 % 2) * 4096:(tt2 % 2) * 4096 + 4096]).then_inc(std_.sem, 16)
                    p.emit("sync", fdr)
                    std_.inc(16)
                pl.slice(tt // 4)
        if p3:
            def ffin3(g, target=st.v):
                g.wait_ge(st.sem, target)
            p.emit("gpsimd", ffin3)
            p.finalize()
            return b.nc

        if standalone:
            def ffinal(s, target=st.v, tgt2=std_.v):
                s.wait_ge(st.sem, target)
                if tgt2:
                    s.wait_ge(std_.sem, tgt2)
            p.emit("sync", ffinal)
            p.finalize()
        else:
            emit_phase_barrier(b, st)
    return b.nc


def stage_l0attn(inputs):
    k0 = np.asarray(inputs["L0_attn_k0"], np.float32)[0]
    k1 = np.asarray(inputs["L0_attn_k1"], np.float32)[0]
    hk1 = np.zeros((D, 384), np.float32)
    hk1[:, 128:256] = k0[:, ::-1]
    hk2 = np.zeros((D, 384), np.float32)
    hk2[:, 0:127] = k1[:, 1:128]
    hk2[:, 255:383] = k1
    return {
        "winA": np.ascontiguousarray(np.asarray(inputs["L0_attn_W_in"], np.float32)),
        "bin_r": np.asarray(inputs["L0_attn_b_in"], np.float32).reshape(1, -1).copy(),
        "hk1": hk1, "hk2": hk2,
        "gamv": np.asarray(inputs["L0_attn_gamma"], np.float32).reshape(-1, 1).copy(),
        "betv": np.asarray(inputs["L0_attn_beta"], np.float32).reshape(-1, 1).copy(),
        "wout": np.ascontiguousarray(np.asarray(inputs["L0_attn_W_out"], np.float32)),
        "bout": np.asarray(inputs["L0_attn_b_out"], np.float32).reshape(-1, 1).copy(),
        "consts": host_consts(),
    }


QMIDS = [(0.25, 0.5), (0.75, 0.5), (1.25, 0.5), (1.75, 0.5),
         (2.5, 1.0), (3.5, 1.0), (5.0, 2.0)]


def emit_quant(b, x_f32, out_bf, width, scr, qn="q"):
    p = b.p
    nb = width // 16
    a, g1, g2, sgn = scr["a"], scr["g1"], scr["g2"], scr["sgn"]
    s_, rs = scr["s"], scr["rs"]
    qsy = b.sem("qsy")
    qvd = b.sem("qvd")
    qvd_prev = qvd.v

    def fabs(s, qvd_prev=qvd_prev):
        if qvd_prev > 0:
            s.wait_ge(qvd.sem, qvd_prev)
        s.activation(out=a[:, 0:width], in_=x_f32, func=AF.Abs)
        s.drain()
        s.activation(out=sgn[:, 0:width], in_=x_f32,
                     func=AF.Sign).then_inc(qsy.sem, 1)
    p.emit("scalar", fabs, reads=[x_f32], writes=[a[:], sgn[:]])
    qsv = qsy.inc()

    def fq(v, qsv=qsv):
        v.wait_ge(qsy.sem, qsv)
        v.tensor_reduce(out=s_[:, 0:nb],
                        in_=x_f32.rearrange("p (n b) -> p n b", b=16),
                        axis=mybir.AxisListType.X, op=ALU.max,
                        apply_absolute_value=True)
        v.drain()
        v.tensor_scalar(out=s_[:, 0:nb], in0=s_[:, 0:nb], scalar1=1.0 / 6.0,
                        scalar2=1e-12, op0=ALU.mult, op1=ALU.max)
        v.drain()
        v.reciprocal(out=rs[:, 0:nb], in_=s_[:, 0:nb])
        v.drain()
        rs_b = rs[:, 0:nb].unsqueeze(2).broadcast_to((128, nb, 16))
        v.tensor_tensor(out=a[:, 0:width].rearrange("p (n b) -> p n b", b=16),
                        in0=a[:, 0:width].rearrange("p (n b) -> p n b", b=16),
                        in1=rs_b, op=ALU.mult)
        v.drain()
        v.tensor_scalar(out=g1[:, 0:width], in0=a[:, 0:width], scalar1=0.25,
                        scalar2=None, op0=ALU.is_gt)
        for mid in (0.75, 1.25, 1.75):
            v.drain()
            v.scalar_tensor_tensor(out=g1[:, 0:width], in0=a[:, 0:width],
                                   scalar=mid, in1=g1[:, 0:width],
                                   op0=ALU.is_gt, op1=ALU.add)
        v.tensor_scalar(out=g2[:, 0:width], in0=a[:, 0:width], scalar1=2.5,
                        scalar2=None, op0=ALU.is_gt)
        v.drain()
        v.scalar_tensor_tensor(out=g2[:, 0:width], in0=a[:, 0:width],
                               scalar=3.5, in1=g2[:, 0:width],
                               op0=ALU.is_gt, op1=ALU.add)
        v.drain()
        v.scalar_tensor_tensor(out=g2[:, 0:width], in0=g1[:, 0:width],
                               scalar=0.5, in1=g2[:, 0:width],
                               op0=ALU.mult, op1=ALU.add)
        v.tensor_scalar(out=g1[:, 0:width], in0=a[:, 0:width], scalar1=5.0,
                        scalar2=2.0, op0=ALU.is_gt, op1=ALU.mult)
        v.drain()
        v.tensor_tensor(out=g2[:, 0:width], in0=g2[:, 0:width],
                        in1=g1[:, 0:width], op=ALU.add)
        v.drain()
        v.tensor_tensor(out=g2[:, 0:width], in0=g2[:, 0:width],
                        in1=sgn[:, 0:width], op=ALU.mult)
        v.drain()
        s_b = s_[:, 0:nb].unsqueeze(2).broadcast_to((128, nb, 16))
        v.tensor_tensor(out=out_bf.rearrange("p (n b) -> p n b", b=16),
                        in0=g2[:, 0:width].rearrange("p (n b) -> p n b", b=16),
                        in1=s_b, op=ALU.mult)
        v.then_inc_done = None
    p.emit("vector", fq, reads=[x_f32, a[:], sgn[:]],
           writes=[out_bf, a[:], g1[:], g2[:], s_[:], rs[:]])

    def fqd(v):
        v.nop().then_inc(qvd.sem, 1)
    p.emit("vector", fqd)
    qvd.inc()


def build_l1ffn(b=None):
    standalone = b is None
    if standalone:
        b = Build("l1ffn")
    p = b.p
    x_cm = b.dram_in("x_cm", [D, T])
    wg = b.dram_in("wg", [2816, D])
    wu = b.dram_in("wu", [2816, D])
    wd = b.dram_in("wd", [D, 2816])
    x_out = b.dram_out("x_out", [D, T])
    I = 2816
    NIT = 22

    with (b.stack if standalone else contextlib.nullcontext()):
        if standalone:
            b.psum_arena()
            common_consts(b)
        ldc = b.ldc
        wgq = b.sb("wgq", [128, NKT * I], BF16)
        wuq = b.sb("wuq", [128, NKT * I], BF16)
        wdq = b.sb("wdq", [128, NIT * D], BF16)
        wrow = b.sb("wrow", [128, 1024], F32)
        wqrow = b.sb("wqrow", [128, 1024], BF16)
        qa = b.sb("qa", [128, 1024], F32)
        qg1 = b.sb("qg1", [128, 1024], BF16)
        qg2 = b.sb("qg2", [128, 1024], BF16)
        qsgn = b.sb("qsgn", [128, 1024], BF16)
        qs = b.sb("qs", [128, 64], F32)
        qrs = b.sb("qrs", [128, 64], F32)
        scr = {"a": qa, "g1": qg1, "g2": qg2, "sgn": qsgn, "s": qs, "rs": qrs}
        xf = b.sb("xf", [128, D], F32)
        xfc = wrow
        identf2 = b.sb("identf2", [128, 128], F32)
        xq = b.sb("xq", [128, D], BF16)
        xqcm = b.sb("xqcm", [128, 2 * NKT * 128], BF16)
        hbuf = b.sb("hbuf", [128, 1024], F32)
        sgbuf = b.sb("sgbuf", [128, 512], F32)
        hq = b.sb("hq", [128, I], BF16)
        hqcm = b.sb("hqcm", [128, NIT * 128], BF16)
        xres = b.sb("xres2", [128, 2 * NKT * 128], BF16)
        rsl = b.sb("rsl2", [128, 2 * NKT * 512], BF16)

        ld = ldc
        ldx = ldc
        ldr = ldc
        wqd = b.sem("wqd")
        wtr = b.sem("wtr")
        wte = b.sem("wte")
        xqd = b.sem("xqd")
        xtr = b.sem("xtr")
        xte = b.sem("xte")
        gum = b.sem("gum")
        gue = b.sem("gue")
        hqd = b.sem("hqd")
        htr = b.sem("htr")
        hte = b.sem("hte")
        dwm = b.sem("dwm")
        hqk = b.sem("hqk")
        sgd = b.sem("sgd")
        dwe = b.sem("dwe")
        xt2 = b.sem("xt2")
        xe2 = b.sem("xe2")
        st = b.sem("st")

        def fidc(v, _ldc=ldc):
            v.wait_ge(_ldc.sem, 64)
            v.tensor_copy(out=identf2[:], in_=b.ident[:]).then_inc(xe2.sem, 1)
        p.emit("vector", fidc, reads=[b.ident[:]], writes=[identf2[:]])
        xe2.inc()

        wline = [("wg", wg, 22, 1024, wgq), ("wu", wu, 22, 1024, wuq),
                 ("wd", wd, 8, 2816, wdq)]
        trcnt = 0
        for wi, (nm, src, ntile, wid, dstq) in enumerate(wline):
            for it in range(ntile):
                for c0 in range(0, wid, 1024):
                    cw = min(1024, wid - c0)
                    wqv_prev = wqd.v

                    def flw(g, src=src, it=it, c0=c0, cw=cw, wqv_prev=wqv_prev,
                            _first=(wi == 0 and it == 0 and c0 == 0),
                            _ph=b.ph, _pn=b.phase, _fu=b.fused):
                        if _first and _fu and _pn > 0:
                            g.wait_ge(_ph.sem, _pn)
                        g.wait_ge(wqd.sem, wqv_prev)
                        g.dma_start(out=wrow[:, 0:cw],
                                    in_=src.ap()[it * 128:(it + 1) * 128,
                                                 c0:c0 + cw]).then_inc(ld.sem, 16)
                    p.emit("gpsimd", flw)
                    ld.inc(16)
                    ldv = ld.v

                    def fwait(v, ldv=ldv):
                        v.wait_ge(ld.sem, ldv)
                    p.emit("vector", fwait)

                    def fwait_s(s, ldv=ldv):
                        s.wait_ge(ld.sem, ldv)
                    p.emit("scalar", fwait_s)
                    emit_quant(b, wrow[:, 0:cw], wqrow[:, 0:cw], cw, scr)

                    def fdone(v):
                        v.nop().then_inc(wqd.sem, 1)
                    p.emit("vector", fdone)
                    wqv = wqd.inc()

                    for ct in range(cw // 128):
                        tidx = trcnt
                        trcnt += 1
                        kidx = (c0 // 128) + ct

                        def ftr(t, ct=ct, tidx=tidx, wqv=wqv):
                            t.wait_ge(wqd.sem, wqv)
                            if tidx == 0:
                                t.wait_ge(ldc.sem, 64)
                            if tidx >= 2:
                                t.wait_ge(wte.sem, tidx - 1)
                            t.transpose(b.bank_bf(tidx % 2, 128),
                                        wqrow[:, ct * 128:(ct + 1) * 128],
                                        b.ident[:]).then_inc(wtr.sem, 1)
                        p.emit("tensor", ftr)
                        trv = wtr.inc()

                        if wid == 1024:
                            dst = dstq[:, kidx * I + it * 128: kidx * I + it * 128 + 128]
                        else:
                            dst = dstq[:, kidx * D + it * 128: kidx * D + it * 128 + 128]

                        def fte(v, dst=dst, tidx=tidx, trv=trv):
                            v.wait_ge(wtr.sem, trv)
                            v.tensor_copy(out=dst, in_=b.bank_bf(tidx % 2, 128)).then_inc(wte.sem, 1)
                        p.emit("vector", fte, writes=[dstq[:]])
                        wte.inc()
        WTE_ALL = wte.v

        pl = make_postln(
            b, rsl, x_out, st, dwe, lambda tt2: (tt2 + 1) * 4 * NKT,
            stat_banks=(6, 7), pref="f",
            rcol=lambda kt, tt2: (tt2 % 2) * NKT * 512 + kt * 512,
            bufs={"sq": wqrow})
        pln = pl.sems

        for tt in range(NTT):
            tt2 = tt // 4
            slot = tt % 2
            for kt in range(NKT):
                idx2 = tt * NKT + kt

                def flx2(g, kt=kt, tt=tt, idx2=idx2, xt2v=xt2.v, wqall=wqd.v):
                    if idx2 < 2:
                        g.wait_ge(wqd.sem, wqall)
                    else:
                        g.wait_ge(xt2.sem, xt2v - 1)
                    g.dma_start(out=xfc[:, (idx2 % 2) * 128:(idx2 % 2) * 128 + 128],
                                in_=x_cm.ap()[kt * 128:(kt + 1) * 128,
                                              tt * 128:(tt + 1) * 128]).then_inc(ldx.sem, 16)
                p.emit("gpsimd", flx2)
                ldx.inc(16)
                lx2 = ldx.v

                def ftx2(t, kt=kt, idx2=idx2, lx2=lx2, xe2v=xe2.v,
                         wteall=wte.v, xtev=xte.v, htev=hte.v):
                    t.wait_ge(ldx.sem, lx2)
                    if idx2 == 0:
                        t.wait_ge(xe2.sem, 1)
                        t.wait_ge(wte.sem, wteall)
                    if kt == 0 and idx2 > 0:
                        t.wait_ge(xte.sem, xtev)
                        t.wait_ge(hte.sem, htev)
                    if idx2 >= 2:
                        t.wait_ge(xe2.sem, xe2v - 1)
                    t.transpose(b.bank(idx2 % 2)[:, 128:256],
                                xfc[:, (idx2 % 2) * 128:(idx2 % 2) * 128 + 128],
                                identf2[:]).then_inc(xt2.sem, 1)
                p.emit("tensor", ftx2)
                t2v = xt2.inc()

                def fxe2(v, kt=kt, tt=tt, idx2=idx2, t2v=t2v):
                    v.wait_ge(xt2.sem, t2v)
                    if kt == 0 and tt >= 1:
                        v.wait_ge(xqd.sem, tt)
                    v.tensor_copy(out=xf[:, kt * 128:(kt + 1) * 128],
                                  in_=b.bank(idx2 % 2)[:, 128:256]).then_inc(xe2.sem, 1)
                p.emit("vector", fxe2, writes=[xf[:]])
                xe2.inc()
            lxv = xe2.v

            for kt in range(NKT):
                def flr(g, kt=kt, tt=tt, slot=slot):
                    if kt == 0 and tt >= 2:
                        g.wait_ge(dwe.sem, (tt - 1) * NKT)
                    g.dma_start(out=xres[:, slot * NKT * 128 + kt * 128:
                                         slot * NKT * 128 + kt * 128 + 128],
                                in_=x_cm.ap()[kt * 128:(kt + 1) * 128,
                                              tt * 128:tt * 128 + 128]).then_inc(ldr.sem, 16)
                p.emit("gpsimd", flr)
                ldr.inc(16)
            lrv = ldr.v

            def fwv(v, lxv=lxv, tt=tt):
                v.wait_ge(xe2.sem, lxv)
                if tt == 0:
                    v.wait_ge(wte.sem, WTE_ALL)
                elif tt >= 1:
                    v.wait_ge(xtr.sem, tt * NKT)
            p.emit("vector", fwv)

            def fws(s, lxv=lxv):
                s.wait_ge(xe2.sem, lxv)
            p.emit("scalar", fws)
            emit_quant(b, xf[:, 0:D], xq[:, 0:D], D, scr)

            def fxq(v):
                v.nop().then_inc(xqd.sem, 1)
            p.emit("vector", fxq)
            xqv = xqd.inc()

            for kt in range(NKT):
                idx = tt * NKT + kt

                def ftx(t, kt=kt, tt=tt, slot=slot, idx=idx, xqv=xqv):
                    t.wait_ge(xqd.sem, xqv)
                    if idx >= 2:
                        t.wait_ge(xte.sem, idx - 1)
                    t.transpose(b.bank_bf(idx % 2, 128),
                                xq[:, kt * 128: kt * 128 + 128],
                                b.ident[:]).then_inc(xtr.sem, 1)
                p.emit("tensor", ftx)
                txv = xtr.inc()

                def fxe(v, kt=kt, slot=slot, idx=idx, txv=txv):
                    v.wait_ge(xtr.sem, txv)
                    if idx >= 2 and kt == 0:
                        v.wait_ge(gum.sem, (idx // NKT - 1) * 6)
                    v.tensor_copy(out=xqcm[:, slot * NKT * 128 + kt * 128:
                                           slot * NKT * 128 + kt * 128 + 128],
                                  in_=b.bank_bf(idx % 2, 128)).then_inc(xte.sem, 1)
                p.emit("vector", fxe, writes=[xqcm[:]])
                xte.inc()
            xtev = xte.v

            isl = [(0, 512), (512, 512), (1024, 512), (1536, 512),
                   (2048, 512), (2560, 256)]
            for si, (i0, iw) in enumerate(isl):
                def fgu(t, tt=tt, slot=slot, si=si, i0=i0, iw=iw, xtev=xtev):
                    if si == 0:
                        t.wait_ge(xte.sem, xtev)
                        if tt > 0:
                            t.wait_ge(gue.sem, tt * 6)
                    else:
                        t.wait_ge(gue.sem, tt * 6 + si)
                    for kt in range(NKT):
                        t.matmul(b.bank(2, iw),
                                 xqcm[:, slot * NKT * 128 + kt * 128:
                                      slot * NKT * 128 + kt * 128 + 128],
                                 wgq[:, kt * I + i0: kt * I + i0 + iw],
                                 start=(kt == 0), stop=(kt == NKT - 1))
                    for kt in range(NKT):
                        mm = t.matmul(b.bank(3, iw),
                                      xqcm[:, slot * NKT * 128 + kt * 128:
                                           slot * NKT * 128 + kt * 128 + 128],
                                      wuq[:, kt * I + i0: kt * I + i0 + iw],
                                      start=(kt == 0), stop=(kt == NKT - 1))
                    mm.then_inc(gum.sem, 1)
                p.emit("tensor", fgu)
                gv = gum.inc()

                def fsl(s, si=si, i0=i0, iw=iw, gv=gv, guev=gue.v):
                    s.wait_ge(gum.sem, gv)
                    if guev > 0:
                        s.wait_ge(gue.sem, guev)
                    s.activation(out=sgbuf[:, 0:iw], in_=b.bank(2, iw),
                                 func=AF.Silu).then_inc(sgd.sem, 1)
                p.emit("scalar", fsl, writes=[sgbuf[:]])
                sgv = sgd.inc()

                def fmu(v, si=si, i0=i0, iw=iw, gv=gv, tt=tt, sgv=sgv):
                    v.wait_ge(gum.sem, gv)
                    v.wait_ge(sgd.sem, sgv)
                    if si >= 2:
                        v.wait_ge(hqk.sem, tt * 3 + si // 2)
                    elif tt > 0:
                        v.wait_ge(hqk.sem, tt * 3)
                    v.tensor_tensor(out=hbuf[:, (si % 2) * 512:(si % 2) * 512 + iw],
                                    in0=sgbuf[:, 0:iw],
                                    in1=b.bank(3, iw), op=ALU.mult).then_inc(gue.sem, 1)
                p.emit("vector", fmu, reads=[sgbuf[:]], writes=[hbuf[:]])
                gue.inc()
                if si % 2 == 1:
                    ck = si // 2
                    cw = 1024 if ck < 2 else 768
                    guev = gue.v

                    def fhw(v, guev=guev):
                        v.wait_ge(gue.sem, guev)
                    p.emit("vector", fhw)

                    def fhs(s, guev=guev):
                        s.wait_ge(gue.sem, guev)
                    p.emit("scalar", fhs)
                    emit_quant(b, hbuf[:, 0:cw], hq[:, ck * 1024:ck * 1024 + cw],
                               cw, scr)

                    def fhq(v):
                        v.nop().then_inc(hqk.sem, 1)
                    p.emit("vector", fhq)
                    hqk.inc()
            hqv = hqk.v

            def fhq2(v, hqv=hqv):
                v.wait_ge(hqk.sem, hqv)
                v.nop().then_inc(hqd.sem, 1)
            p.emit("vector", fhq2)
            hqd.inc()
            hqv = hqd.v

            for it in range(NIT):
                idx = tt * NIT + it

                def fth(t, it=it, tt=tt, slot=slot, idx=idx, hqv=hqv):
                    t.wait_ge(hqd.sem, hqv)
                    if idx >= 2:
                        t.wait_ge(hte.sem, idx - 1)
                    t.transpose(b.bank_bf(idx % 2, 128),
                                hq[:, it * 128:(it + 1) * 128],
                                b.ident[:]).then_inc(htr.sem, 1)
                p.emit("tensor", fth)
                thv = htr.inc()

                def fhe(v, it=it, idx=idx, thv=thv):
                    v.wait_ge(htr.sem, thv)
                    if it == 0 and idx >= NIT:
                        v.wait_ge(dwm.sem, idx // NIT)
                    v.tensor_copy(out=hqcm[:, it * 128: it * 128 + 128],
                                  in_=b.bank_bf(idx % 2, 128)).then_inc(hte.sem, 1)
                p.emit("vector", fhe, writes=[hqcm[:]])
                hte.inc()
            htev = hte.v

            def fdw(t, tt=tt, slot=slot, htev=htev, lrv=lrv):
                t.wait_ge(hte.sem, htev)
                t.wait_ge(ldr.sem, lrv)
                if tt >= 2:
                    t.wait_ge(dwe.sem, (tt - 1) * NKT)
                for oc in range(NKT):
                    bank = 4 + oc // 4
                    coff = (oc % 4) * 128
                    for it in range(NIT):
                        t.matmul(b.bank(bank)[:, coff:coff + 128],
                                 wdq[:, it * D + oc * 128: it * D + oc * 128 + 128],
                                 hqcm[:, it * 128: it * 128 + 128],
                                 start=(it == 0), stop=(it == NIT - 1))
                    mm = t.matmul(b.bank(bank)[:, coff:coff + 128], b.ident[:],
                                  xres[:, slot * NKT * 128 + oc * 128:
                                       slot * NKT * 128 + oc * 128 + 128],
                                  start=False, stop=True)
                mm.then_inc(dwm.sem, 1)
            p.emit("tensor", fdw)
            dv = dwm.inc()

            for oc in range(NKT):
                def fde(v, oc=oc, tt=tt, tt2=tt2, slot=slot, dv=dv):
                    if oc == 0:
                        v.wait_ge(dwm.sem, dv)
                        if tt % 4 == 0 and tt2 >= 2:
                            v.wait_ge(pln['ap_done'].sem, (tt2 - 1) * NKT)
                            v.wait_ge(pln['sq_cons'].sem, (tt2 - 1) * NKT)
                    bank = 4 + oc // 4
                    coff = (oc % 4) * 128
                    v.tensor_copy(
                        out=rsl[:, (tt2 % 2) * NKT * 512 + oc * 512 + (tt % 4) * 128:
                                (tt2 % 2) * NKT * 512 + oc * 512 + (tt % 4) * 128 + 128],
                        in_=b.bank(bank)[:, coff:coff + 128]).then_inc(dwe.sem, 1)
                p.emit("vector", fde, writes=[rsl[:]])
                dwe.inc()
            if tt % 4 == 3:
                pl.slice(tt // 4)

        if standalone:
            def ffinal(s, target=st.v):
                s.wait_ge(st.sem, target)
            p.emit("sync", ffinal)
            p.finalize()
        else:
            emit_phase_barrier(b, st)
    return b.nc


def stage_l1ffn(inputs):
    return {
        "wg": np.ascontiguousarray(np.asarray(inputs["L1_ffn_w_gate"], np.float32)),
        "wu": np.ascontiguousarray(np.asarray(inputs["L1_ffn_w_up"], np.float32)),
        "wd": np.ascontiguousarray(np.asarray(inputs["L1_ffn_w_down"], np.float32)),
        "consts": host_consts(),
    }


def build_l1attn(b=None, stop_after=None):
    standalone = b is None
    if standalone:
        b = Build("l1attn")
    p = b.p
    x_cm = b.dram_in("x_cm", [D, T])
    dbg = b.dram_out("dbg", [128, 8192]) if stop_after else None
    qkvwT = b.dram_in("qkvwT", [D, 3 * D])
    ctrlpT = b.dram_in("ctrlpT", [D, 1152])
    ctrlbc = b.dram_in("ctrlbc", [128, 9])
    convw = b.dram_in("convw", [128, NKT * 12])
    nwtile = b.dram_in("nwtile", [128, 64])
    outwT = b.dram_in("outwT", [D, D])
    outb = b.dram_in("outb", [D, 1])
    cst2 = b.dram_in("cst2", [128, 1536])
    x_out = b.dram_out("x_out", [D, T])

    BT = 512
    NBC = T // BT
    NCH = BT // 64
    NPR = 8
    NTB = BT // 128
    NU = NPR * NCH
    EPS = 1e-6

    with (b.stack if standalone else contextlib.nullcontext()):
        if standalone:
            b.psum_arena()
            common_consts(b)
        ldc = b.ldc
        mkKrg = b.sb("mkKrg", [128, 64], F32)
        mkKrl = b.sb("mkKrl", [128, 64], F32)
        s01U = b.sb("s01U", [128, 64], F32)
        s01L = b.sb("s01L", [128, 64], F32)
        identf = b.sb("identf", [128, 128], F32)
        nwb = b.sb("nwb", [128, 64], BF16)
        outb_s = b.sb("outb_s", [128, NKT], F32)
        cw_s = b.sb("cw_s", [128, NKT * 12], F32)
        ctb_s = b.sb("ctb_s", [128, 9], F32)
        hselq = b.sb("hselq", [128, 128], F32)
        hselt = b.sb("hselt", [16, 1024], F32)
        wsl = b.sb("wsl2", [128, 2 * 1024], BF16)
        outw_s = b.sb("outw_s", [128, NKT * D], BF16)
        x_bf = b.sb("x_bf2", [128, NKT * BT], BF16)
        q_cm = b.sb("q_cm", [128, NKT * BT], BF16)
        k_cm = b.sb("k_cm", [128, NKT * BT], BF16)
        v_cm = b.sb("v_cm", [128, NKT * BT], BF16)
        gsig = b.sb("gsig", [128, NKT * BT], BF16)
        kTc = b.sb("kTc", [128, NU * 64], BF16)
        halo = b.sb("halo", [128, 24 * 3], F32)
        cbuf = b.sb("cbuf", [128, BT + 8], F32)
        cbuf2 = b.sb("cbuf2", [128, BT + 8], F32)
        sqb2 = b.sb("sqb2", [128, BT], F32)
        abg = b.sb("abg", [128, BT], F32)
        la_r = b.sb("la_r", [16, BT], F32)
        g_r = b.sb("g_r", [16, BT], F32)
        ones16 = b.sb("ones16", [16, 64], F32)
        ones4f = b.sb("ones4f", [128, 64], F32)
        rq_b = b.sb("rq_b", [16, 2 * BT], F32)
        mgbB = b.sb("mgbB", [128, 1024], F32)
        gcol = b.sb("gcol", [64, 16 * NCH], F32)
        bcolh = b.sb("bcolh", [64, 16 * NCH], F32)
        gcolP = b.sb("gcolP", [128, NU], F32)
        bcolP = b.sb("bcolP", [128, NU], F32)
        mbcolP = b.sb("mbcolP", [128, NU], F32)
        gacolP = b.sb("gacolP", [128, NU], F32)
        ngbcolP = b.sb("ngbcolP", [128, NU], F32)
        dcolP = b.sb("dcolP", [128, NU], F32)
        Ubuf = b.sb("Ubuf", [128, NU * 64], BF16)
        WTbuf = b.sb("WTbuf", [128, NU * 64], BF16)
        Gqbuf = b.sb("Gqbuf", [128, NU * 64], BF16)
        tscrB = b.sb("tscrB", [128, 512], F32)
        gamTB = b.sb("gamTB", [128, 512], F32)
        gamUB = b.sb("gamUB", [128, 512], F32)
        nbufB = b.sb("nbufB", [128, 2 * 512], BF16)
        ntbufB = b.sb("ntbufB", [128, 2 * 512], BF16)
        xsolB = b.sb("xsolB", [128, 2 * 1024], BF16)
        zhatB = b.sb("zhatB", [128, 512], BF16)
        zprB = b.sb("zprB", [128, 512], BF16)
        rt1B = b.sb("rt1B", [128, 512], F32)
        rtm = b.sb("rtm", [128, NU * 64], BF16)
        rsq = b.sb("rsq", [128, NU], F32)
        ncm = x_bf
        s0f = b.sb("s0f", [128, NPR * 64], F32)
        s0b = b.sb("s0b", [128, NPR * 64], BF16)
        s0sB = b.sb("s0sB", [128, 512], F32)
        gccB = b.sb("gccB", [128, NPR], F32)
        gsc = b.sb("gsc", [128, 32], F32)
        gsc2 = b.sb("gsc2", [128, 16], F32)
        onesW = b.sb("onesW", [128, 512], F32)
        onesB3 = bass_rust.AP(tensor=ones4f[:].tensor, offset=0,
                              ap=[[64, 128], [0, NPR], [1, 64]])
        mkb3g = bass_rust.AP(tensor=mkKrg[:].tensor, offset=0,
                             ap=[[64, 128], [0, NPR], [1, 64]])
        mkb3l = bass_rust.AP(tensor=mkKrl[:].tensor, offset=0,
                             ap=[[64, 128], [0, NPR], [1, 64]])
        s01U3 = bass_rust.AP(tensor=s01U[:].tensor, offset=0,
                             ap=[[64, 128], [0, NPR], [1, 64]])
        s01L3 = bass_rust.AP(tensor=s01L[:].tensor, offset=0,
                             ap=[[64, 128], [0, NPR], [1, 64]])
        r3b = lambda ap_: ap_.rearrange("p (u h) -> p u h", h=64)
        xres = b.sb("xres3", [128, 2 * NKT * 128], BF16)
        rsl = b.sb("rsl3", [128, 2 * NKT * 512], BF16)

        ld = ldc
        ldw = ldc
        ldx = ldc
        pj = b.sem("pj")
        pev = b.sem("pev")
        cvd = b.sem("cvd")
        nqm = b.sem("nqm")
        nqf = b.sem("nqf")
        l2b = b.sem("l2b")
        l2a = b.sem("l2a")
        abd = b.sem("abd")
        scd = b.sem("scd")
        gct = b.sem("gct")
        sA0 = b.sem("sA0")
        gce = b.sem("gce")
        colP = b.sem("colP")
        trk = b.sem("trk")
        tre = b.sem("tre")
        sA1 = b.sem("sA1")
        sA2 = b.sem("sA2")
        sA3 = b.sem("sA3")
        sA4 = b.sem("sA4")
        sB1 = b.sem("sB1")
        sB2 = b.sem("sB2")
        sB3 = b.sem("sB3")
        sB4 = b.sem("sB4")
        rsd = b.sem("rsd")
        rse = b.sem("rse")
        ntr = b.sem("ntr")
        nte = b.sem("nte")
        owm = b.sem("owm")
        owe = b.sem("owe")
        ldr = ldc
        st = b.sem("st")

        def floads(g, _ph=b.ph, _pn=b.phase, _fu=b.fused):
            if _fu and _pn > 0:
                g.wait_ge(_ph.sem, _pn)
            for half in range(2):
                g.dma_start(out=mkKrg[half * 64:half * 64 + 64, :],
                            in_=cst2.ap()[half * 64:half * 64 + 64, 0:64]).then_inc(ld.sem, 16)
                g.dma_start(out=mkKrl[half * 64:half * 64 + 64, :],
                            in_=cst2.ap()[half * 64:half * 64 + 64, 64:128]).then_inc(ld.sem, 16)
            g.dma_start(out=s01U[:], in_=cst2.ap()[:, 128:192]).then_inc(ld.sem, 16)
            g.dma_start(out=s01L[:], in_=cst2.ap()[:, 192:256]).then_inc(ld.sem, 16)
            g.dma_start(out=hselq[:], in_=cst2.ap()[:, 256:384]).then_inc(ld.sem, 16)
            g.dma_start(out=hselt[:], in_=cst2.ap()[0:16, 384:1408]).then_inc(ld.sem, 16)
            g.dma_start(out=identf[:], in_=cst2.ap()[:, 1408:1536]).then_inc(ld.sem, 16)
            g.dma_start(out=nwb[:], in_=nwtile.ap()).then_inc(ld.sem, 16)
            g.dma_start(out=outb_s[:].unsqueeze(2),
                        in_=outb.ap().rearrange("(a p) o -> p a o", p=128),
                        allow_slow_non_contiguous=True).then_inc(ld.sem, 16)
            g.dma_start(out=cw_s[:], in_=convw.ap()).then_inc(ld.sem, 16)
            g.dma_start(out=ctb_s[:], in_=ctrlbc.ap()).then_inc(ld.sem, 16)
            for kt in range(NKT):
                g.dma_start(out=outw_s[:, kt * D:(kt + 1) * D],
                            in_=outwT.ap()[kt * 128:(kt + 1) * 128, :]).then_inc(ld.sem, 16)
        p.emit("gpsimd", floads)
        ld.inc(16 * (13 + NKT))
        LDALL = ld.v

        def fmz(v, _ph=b.ph, _pn=b.phase, _fu=b.fused):
            if _fu and _pn > 0:
                v.wait_ge(_ph.sem, _pn)
            v.memset(halo[:], 0.0)
            v.memset(s0f[:], 0.0)
            v.memset(s0b[:], 0.0)
            v.memset(ones16[:], 1.0)
            v.memset(onesW[:], 1.0)
            v.memset(ones4f[:], 1.0).then_inc(abd.sem, 1)
        p.emit("vector", fmz)
        MZ = abd.inc()

        pl = make_postln(
            b, rsl, x_out, st, owe, lambda tt2: (tt2 + 1) * 4 * NKT,
            stat_banks=(6, 7), pref="a",
            rcol=lambda kt, tt2: (tt2 % 2) * NKT * 512 + kt * 512)
        pln = pl.sems
        silu_cv = [0]

        for bc in range(NBC):
            for kt in range(NKT):
                def flx(g, kt=kt, bc=bc):
                    if kt == 0 and bc > 0:
                        g.wait_ge(pj.sem, 33 * bc)
                        g.wait_ge(owm.sem, 4 * bc)
                    g.dma_start(out=x_bf[:, kt * BT:(kt + 1) * BT],
                                in_=x_cm.ap()[kt * 128:(kt + 1) * 128,
                                              bc * BT:(bc + 1) * BT]).then_inc(ldx.sem, 16)
                p.emit("gpsimd", flx)
                ldx.inc(16)
            lxv = ldx.v

            for ot in range(33):
                wsrc, wcol = (qkvwT, ot * 128) if ot < 24 else (ctrlpT, (ot - 24) * 128)
                gi = bc * 33 + ot

                def flw(g, wsrc=wsrc, wcol=wcol, gi=gi, pjv=pj.v):
                    if gi >= 2:
                        g.wait_ge(pj.sem, pjv - 1)
                    for kt in range(NKT):
                        g.dma_start(out=wsl[:, (gi % 2) * 1024 + kt * 128:
                                            (gi % 2) * 1024 + kt * 128 + 128],
                                    in_=wsrc.ap()[kt * 128:(kt + 1) * 128,
                                                  wcol:wcol + 128]).then_inc(ldw.sem, 16)
                p.emit("gpsimd", flw)
                ldw.inc(16 * NKT)
                lwv = ldw.v

                def fpj(t, gi=gi, ot=ot, bc=bc, lwv=lwv, lxv=lxv):
                    t.wait_ge(ldw.sem, lwv)
                    t.wait_ge(ldx.sem, lxv)
                    if gi == 0:
                        t.wait_ge(ld.sem, LDALL)
                        t.wait_ge(ldc.sem, 64)
                    if ot == 0 and bc > 0:
                        t.wait_ge(owe.sem, bc * 4 * NKT)
                    if gi >= 2:
                        t.wait_ge(pev.sem, gi - 1)
                    mm = None
                    for kt in range(NKT):
                        mm = t.matmul(b.bank(gi % 2),
                                      wsl[:, (gi % 2) * 1024 + kt * 128:
                                          (gi % 2) * 1024 + kt * 128 + 128],
                                      x_bf[:, kt * BT:(kt + 1) * BT],
                                      start=(kt == 0), stop=(kt == NKT - 1))
                    mm.then_inc(pj.sem, 1)
                p.emit("tensor", fpj)
                pjv = pj.inc()

                if ot < 24:
                    ten = ot // 8
                    kt_ = ot % 8
                    dst = (q_cm, k_cm, v_cm)[ten]

                    def fev(v, gi=gi, pjv=pjv, ot=ot, bc=bc):
                        v.wait_ge(pj.sem, pjv)
                        if ot == 0 and bc == 0:
                            v.wait_ge(abd.sem, MZ)
                        v.tensor_copy(out=cbuf[:, 0:3], in_=halo[:, ot * 3:ot * 3 + 3])
                        v.drain()
                        v.tensor_copy(out=cbuf[:, 3:3 + BT],
                                      in_=b.bank(gi % 2)).then_inc(pev.sem, 1)
                    p.emit("vector", fev, reads=[halo[:]], writes=[cbuf[:]])
                    pev.inc(1)

                    def fcv(v, ten=ten, kt_=kt_, ot=ot, scv_=silu_cv[0]):
                        if scv_ > 0:
                            v.wait_ge(cvd.sem, scv_)
                        base = kt_ * 12 + ten * 4
                        v.tensor_scalar(out=cbuf2[:, 0:BT], in0=cbuf[:, 3:3 + BT],
                                        scalar1=cw_s[:, base + 3:base + 4],
                                        scalar2=None, op0=ALU.mult)
                        for j in (1, 2, 3):
                            v.drain()
                            v.scalar_tensor_tensor(out=cbuf2[:, 0:BT],
                                                   in0=cbuf[:, 3 - j:3 - j + BT],
                                                   scalar=cw_s[:, base + 3 - j:base + 4 - j],
                                                   in1=cbuf2[:, 0:BT],
                                                   op0=ALU.mult, op1=ALU.add)
                        v.drain()
                        v.tensor_copy(out=halo[:, ot * 3:ot * 3 + 3],
                                      in_=cbuf[:, BT:BT + 3]).then_inc(cvd.sem, 1)
                    p.emit("vector", fcv, reads=[cbuf[:], cbuf2[:], cw_s[:]],
                           writes=[cbuf2[:], halo[:]])
                    cvv = cvd.inc()

                    def fsl(s, dst=dst, kt_=kt_, cvv=cvv):
                        s.wait_ge(cvd.sem, cvv)
                        s.activation(out=dst[:, kt_ * BT:(kt_ + 1) * BT],
                                     in_=cbuf2[:, 0:BT],
                                     func=AF.Silu).then_inc(cvd.sem, 1)
                    p.emit("scalar", fsl, reads=[cbuf2[:]], writes=[dst[:]])
                    cvv2 = cvd.inc()
                    silu_cv[0] = cvv2

                    if ten < 2:
                        def fsq2(s, dst=dst, kt_=kt_, cvv2=cvv2, nqp=nqm.v):
                            s.wait_ge(cvd.sem, cvv2)
                            if nqp > 0:
                                s.wait_ge(nqm.sem, nqp)
                            s.activation(out=sqb2[:, 0:BT],
                                         in_=dst[:, kt_ * BT:(kt_ + 1) * BT],
                                         func=AF.Square).then_inc(cvd.sem, 1)
                        p.emit("scalar", fsq2, reads=[dst[:]], writes=[sqb2[:]])
                        cvv3 = cvd.inc()

                        def fns(t, ten=ten, kt_=kt_, cvv3=cvv3):
                            t.wait_ge(cvd.sem, cvv3)
                            t.matmul(b.psA[0:16, 1024 + ten * 512: 1536 + ten * 512],
                                     hselq[:, kt_ * 16:(kt_ + 1) * 16],
                                     sqb2[:, 0:BT],
                                     start=(kt_ == 0),
                                     stop=(kt_ == 7)).then_inc(nqm.sem, 1)
                        p.emit("tensor", fns)
                        nqm.inc()
                else:
                    ct = ot - 24

                    def fevc(s, gi=gi, pjv=pjv, ct=ct):
                        s.wait_ge(pj.sem, pjv)
                        if ct == 0:
                            s.activation(out=abg[0:16, 0:BT],
                                         in_=b.bank(gi % 2)[0:16, :], func=AF.Sigmoid,
                                         bias=ctb_s[0:16, ct:ct + 1])
                            s.drain()
                            s.activation(out=abg[64:80, 0:BT],
                                         in_=b.bank(gi % 2)[64:80, :], func=AF.Sigmoid,
                                         bias=ctb_s[64:80, ct:ct + 1]).then_inc(pev.sem, 1)
                        else:
                            s.activation(out=gsig[:, (ct - 1) * BT:ct * BT],
                                         in_=b.bank(gi % 2), func=AF.Sigmoid,
                                         bias=ctb_s[:, ct:ct + 1]).then_inc(pev.sem, 1)
                    p.emit("scalar", fevc, writes=[abg[:], gsig[:]])
                    pev.inc(1)
                    if ct == 0:
                        abv0 = pev.v

                        def fa2(v, abv0=abv0):
                            v.wait_ge(pev.sem, abv0)
                            v.tensor_scalar(out=abg[0:16, 0:BT], in0=abg[0:16, 0:BT],
                                            scalar1=0.98, scalar2=0.01,
                                            op0=ALU.mult, op1=ALU.add).then_inc(abd.sem, 1)
                        p.emit("vector", fa2, reads=[abg[:]], writes=[abg[:]])
                        abd.inc()

            NQ_ALL = nqm.v

            def fnq(v, nq=NQ_ALL):
                v.wait_ge(nqm.sem, nq)
                for ten in range(2):
                    v.tensor_single_scalar(out=rq_b[:, ten * BT:(ten + 1) * BT],
                                           in_=b.psA[0:16, 1024 + ten * 512:
                                                     1536 + ten * 512],
                                           scalar=1e-12, op=ALU.max)
                    v.drain()
                v.nop().then_inc(nqf.sem, 1)
            p.emit("vector", fnq, writes=[rq_b[:]])
            nf1 = nqf.inc()

            def fnq2(s, nf1=nf1):
                s.wait_ge(nqf.sem, nf1)
                s.activation(out=rq_b[:], in_=rq_b[:], func=AF.Ln)
                s.drain()
                s.activation(out=rq_b[:], in_=rq_b[:], func=AF.Exp,
                             scale=-0.5).then_inc(nqf.sem, 1)
            p.emit("scalar", fnq2, reads=[rq_b[:]], writes=[rq_b[:]])
            nf2 = nqf.inc()

            for ten in range(2):
                dst = (q_cm, k_cm)[ten]
                for kt_ in range(NKT):
                    li = ten * NKT + kt_

                    def fl2m(t, ten=ten, kt_=kt_, li=li, nf2=nf2, l2av=l2a.v):
                        if li == 0:
                            t.wait_ge(nqf.sem, nf2)
                        if li >= 2:
                            t.wait_ge(l2a.sem, l2av - 1)
                        t.matmul(b.bank(2 + li % 2),
                                 hselt[:, kt_ * 128:(kt_ + 1) * 128],
                                 rq_b[0:16, ten * BT:(ten + 1) * BT],
                                 start=True, stop=True).then_inc(l2b.sem, 1)
                    p.emit("tensor", fl2m)
                    lbv = l2b.inc()

                    def fl2a(v, ten=ten, kt_=kt_, dst=dst, li=li, lbv=lbv):
                        v.wait_ge(l2b.sem, lbv)
                        v.tensor_tensor(
                            out=dst[:, kt_ * BT:(kt_ + 1) * BT],
                            in0=dst[:, kt_ * BT:(kt_ + 1) * BT],
                            in1=b.bank(2 + li % 2), op=ALU.mult).then_inc(l2a.sem, 1)
                    p.emit("vector", fl2a, reads=[dst[:]], writes=[dst[:]])
                    l2a.inc()
            L2_ALL = l2a.v
            if stop_after == 'l2':
                def fdump(g, rv=L2_ALL):
                    g.wait_ge(l2a.sem, rv)
                    g.dma_start(out=dbg.ap()[:, 0:4096],
                                in_=k_cm[:, 0:4096]).then_inc(st.sem, 16)
                p.emit("gpsimd", fdump)
                st.inc(16)
                def ffd(s, tgt=st.v):
                    s.wait_ge(st.sem, tgt)
                p.emit("sync", ffd)
                p.finalize()
                return b.nc


            ABD_A = abd.v

            def fla(s, av=ABD_A):
                s.wait_ge(abd.sem, av)
                s.activation(out=la_r[:], in_=abg[0:16, 0:BT],
                             func=AF.Ln).then_inc(abd.sem, 1)
            p.emit("scalar", fla, reads=[abg[:]], writes=[la_r[:]])
            lav = abd.inc()

            def fsc(v, lav=lav):
                v.wait_ge(abd.sem, lav)
                for c in range(NCH):
                    v.tensor_tensor_scan(out=g_r[:, c * 64:(c + 1) * 64],
                                         data0=ones16[:],
                                         data1=la_r[:, c * 64:(c + 1) * 64],
                                         initial=0.0, op0=ALU.mult, op1=ALU.add)
                    v.drain()
                v.nop().then_inc(scd.sem, 1)
            p.emit("vector", fsc, reads=[la_r[:], ones16[:]], writes=[g_r[:]])
            scv = scd.inc()

            for c in range(NCH):
                for which in range(2):
                    idx = c * 2 + which

                    def fgt(t, c=c, which=which, idx=idx, scv=scv, l2all=L2_ALL,
                            gcev=gce.v, cpp=colP.v):
                        if idx == 0:
                            t.wait_ge(scd.sem, scv)
                            t.wait_ge(l2a.sem, l2all)
                            if cpp > 0:
                                t.wait_ge(colP.sem, cpp - 2)
                        if idx >= 2:
                            t.wait_ge(gce.sem, gcev - 1)
                        if which == 0:
                            src, idn = g_r[:, c * 64:(c + 1) * 64], identf[0:16, 0:16]
                        else:
                            src = abg[64:80, c * 64:(c + 1) * 64]
                            idn = identf[64:80, 64:80]
                        t.transpose(b.bank(2 + idx % 2)[0:64, 0:16], src,
                                    idn).then_inc(gct.sem, 1)
                    p.emit("tensor", fgt)
                    gv = gct.inc()

                    def fge(v, c=c, which=which, idx=idx, gv=gv):
                        v.wait_ge(gct.sem, gv)
                        dst = gcol if which == 0 else bcolh
                        dcols = bass_rust.AP(tensor=dst[:].tensor, offset=c,
                                             ap=[[16 * NCH, 64], [NCH, 16]])
                        v.tensor_copy(out=dcols,
                                      in_=b.bank(2 + idx % 2)[0:64, 0:16]).then_inc(gce.sem, 1)
                    p.emit("vector", fge, writes=[gcol[:], bcolh[:]])
                    gce.inc()
            GCE_ALL = gce.v

            def fcp(s, gcev=GCE_ALL, bc=bc, b4all=sB4.v):
                s.wait_ge(gce.sem, gcev)
                if bc > 0:
                    s.wait_ge(sB4.sem, b4all)
                for srcT, dstT in ((gcol, gcolP), (bcolh, bcolP)):
                    for half in range(2):
                        srcap = bass_rust.AP(tensor=srcT[:].tensor, offset=half * NCH,
                                             ap=[[16 * NCH, 64], [2 * NCH, 8], [1, NCH]])
                        dstap = bass_rust.AP(tensor=dstT[:].tensor, offset=half * 64 * NU,
                                             ap=[[NU, 64], [NCH, 8], [1, NCH]])
                        s.dma_start(out=dstap, in_=srcap,
                                    allow_slow_non_contiguous=True).then_inc(colP.sem, 16)
            p.emit("sync", fcp)
            colP.inc(64)
            cpv = colP.v

            def fcd(s, cpv=cpv):
                s.wait_ge(colP.sem, cpv)
                s.activation(out=gacolP[:], in_=gcolP[:],
                             func=AF.Exp).then_inc(colP.sem, 1)
            p.emit("scalar", fcd, reads=[gcolP[:]], writes=[gacolP[:]])
            cdv = colP.inc()

            def fcd2(v, cdv=cdv):
                v.wait_ge(colP.sem, cdv)
                v.tensor_tensor(out=ngbcolP[:], in0=gacolP[:], in1=bcolP[:], op=ALU.mult)
                v.tensor_single_scalar(out=mbcolP[:], in_=bcolP[:], scalar=-1.0,
                                       op=ALU.mult)
                v.drain()
                v.tensor_single_scalar(out=ngbcolP[:], in_=ngbcolP[:], scalar=-1.0,
                                       op=ALU.mult).then_inc(colP.sem, 1)
            p.emit("vector", fcd2, reads=[gacolP[:], bcolP[:]],
                   writes=[ngbcolP[:], mbcolP[:]])
            colP.inc()
            CPALL = colP.v

            for c in range(NCH):
                def fkt(t, c=c, l2all=L2_ALL, gcev=GCE_ALL, trev=tre.v):
                    if c == 0:
                        t.wait_ge(l2a.sem, l2all)
                        t.wait_ge(gce.sem, gcev)
                    if c >= 2:
                        t.wait_ge(tre.sem, trev - 2)
                    bk = 4 if c % 2 == 0 else 2
                    mm = None
                    for pr in range(NPR):
                        for hh in range(2):
                            sl = slice(hh * 64, hh * 64 + 64)
                            mm = t.matmul(
                                b.psA[sl, (bk + pr // 4) * 512 + (pr % 4) * 128:
                                      (bk + pr // 4) * 512 + (pr % 4) * 128 + 128],
                                k_cm[:, pr * BT + c * 64: pr * BT + c * 64 + 64],
                                b.ident[:], start=True, stop=True)
                    mm.then_inc(trk.sem, 1)
                p.emit("tensor", fkt)
                tv = trk.inc()

                def fke(v, c=c, tv=tv):
                    v.wait_ge(trk.sem, tv)
                    bk = 4 if c % 2 == 0 else 2
                    for hh in range(2):
                        for hb in range(2):
                            src = bass_rust.AP(
                                tensor=b.psA[:].tensor,
                                offset=(hh * 64) * 4096 + (bk + hb) * 512 + hh * 64,
                                ap=[[4096, 64], [128, 4], [1, 64]])
                            dst = bass_rust.AP(
                                tensor=kTc[:].tensor,
                                offset=(hh * 64) * (NU * 64) + (hb * 4 * NCH + c) * 64,
                                ap=[[NU * 64, 64], [NCH * 64, 4], [1, 64]])
                            v.tensor_copy(out=dst, in_=src)
                        v.drain()
                    v.nop().then_inc(tre.sem, 2)
                p.emit("vector", fke, writes=[kTc[:]])
                tre.inc(2)
            TRE_ALL = tre.v
            if stop_after == 'ktc':
                def fdump(g, rv=TRE_ALL):
                    g.wait_ge(tre.sem, rv)
                    g.dma_start(out=dbg.ap()[:, 0:4096],
                                in_=kTc[:, 0:4096]).then_inc(st.sem, 16)
                p.emit("gpsimd", fdump)
                st.inc(16)
                def ffd(s, tgt=st.v):
                    s.wait_ge(st.sem, tgt)
                p.emit("sync", ffd)
                p.finalize()
                return b.nc


            gsel = bass_rust.AP(tensor=gcolP[:].tensor, offset=0,
                                ap=[[NU, 128], [NCH, NPR]])

            def gb3(tensor_h, off0, base_jump, n0=NPR, inner=64):
                return bass_rust.AP(tensor=tensor_h, offset=off0,
                                    ap=[[base_jump, 128], [NCH, n0], [0, inner]])

            for c in range(NCH):
                def fA0(v, c=c, cpall=CPALL, sa1p=sA1.v):
                    if c == 0:
                        v.wait_ge(colP.sem, cpall)
                        v.wait_ge(abd.sem, MZ)
                    else:
                        v.wait_ge(sA1.sem, sa1p)
                    for k_, srcT in enumerate((gcolP, bcolP, mbcolP, ngbcolP)):
                        src = bass_rust.AP(tensor=srcT[:].tensor, offset=c,
                                           ap=[[NU, 128], [NCH, NPR]])
                        v.tensor_copy(out=gsc[:, k_ * 8:(k_ + 1) * 8], in_=src)
                    v.drain()
                    v.tensor_tensor(
                        out=mgbB[:, 0:512].rearrange("p (u h) -> p u h", h=64),
                        in0=onesW[:].rearrange("p (u h) -> p u h", h=64),
                        in1=gsc[:, 0:8].unsqueeze(2).broadcast_to((128, 8, 64)),
                        op=ALU.mult)
                    v.tensor_tensor(
                        out=mgbB[:, 512:1024].rearrange("p (u h) -> p u h", h=64),
                        in0=onesW[:].rearrange("p (u h) -> p u h", h=64),
                        in1=gsc[:, 8:16].unsqueeze(2).broadcast_to((128, 8, 64)),
                        op=ALU.mult).then_inc(sA0.sem, 1)
                p.emit("vector", fA0,
                       reads=[onesW[:], gcolP[:], bcolP[:], mbcolP[:],
                              ngbcolP[:], gsc[:]],
                       writes=[mgbB[:], gsc[:]])
                a0v = sA0.inc()
                if c == 0 and stop_after == 'pa0':
                    def fdump(g, rv=a0v):
                        g.wait_ge(sA0.sem, rv)
                        g.dma_start(out=dbg.ap()[:, 0:1024],
                                    in_=mgbB[:, 0:1024]).then_inc(st.sem, 16)
                    p.emit("gpsimd", fdump)
                    st.inc(16)
                    def ffd(s, tgt=st.v):
                        s.wait_ge(st.sem, tgt)
                    p.emit("sync", ffd)
                    p.finalize()
                    return b.nc


                def fA1(t, c=c, a0v=a0v, trall=TRE_ALL, pvh=pev.v, sa2p=sA2.v):
                    t.wait_ge(sA0.sem, a0v)
                    if c == 0:
                        t.wait_ge(tre.sem, trall)
                        t.wait_ge(pev.sem, pvh)
                    else:
                        t.wait_ge(sA2.sem, sa2p)
                    mm = None
                    for pr in range(NPR):
                        u = pr * NCH + c
                        kts = k_cm[:, pr * BT + c * 64: pr * BT + c * 64 + 64]
                        qts = q_cm[:, pr * BT + c * 64: pr * BT + c * 64 + 64]
                        vts = v_cm[:, pr * BT + c * 64: pr * BT + c * 64 + 64]
                        for hh in range(2):
                            sl = slice(hh * 64, hh * 64 + 64)
                            t.matmul(b.psA[sl, 0 * 512 + pr * 64: 0 * 512 + pr * 64 + 64],
                                     kts[sl, :], qts[sl, :], start=True, stop=True)
                            t.matmul(b.psA[sl, 1 * 512 + pr * 64: 1 * 512 + pr * 64 + 64],
                                     kts[sl, :], kts[sl, :], start=True, stop=True)
                            t.matmul(b.psA[sl, 2 * 512 + pr * 64: 2 * 512 + pr * 64 + 64],
                                     mgbB[sl, pr * 64: pr * 64 + 64],
                                     identf[sl, sl], start=True, stop=True)
                            t.matmul(b.psA[sl, 3 * 512 + pr * 64: 3 * 512 + pr * 64 + 64],
                                     mgbB[sl, 512 + pr * 64: 512 + pr * 64 + 64],
                                     identf[sl, sl], start=True, stop=True)
                            mm = t.matmul(b.psA[sl, 4 * 512 + pr * 64: 4 * 512 + pr * 64 + 64],
                                          vts[sl, :], b.ident[sl, sl],
                                          start=True, stop=True)
                    mm.then_inc(sA1.sem, 1)
                p.emit("tensor", fA1)
                a1v = sA1.inc()
                if c == 0 and stop_after == 'pa1':
                    def fdump(g, rv=a1v):
                        g.wait_ge(sA1.sem, rv)
                        g.dma_start(out=dbg.ap()[:, 0:1024],
                                    in_=mgbB[:, 0:1024]).then_inc(st.sem, 16)
                    p.emit("gpsimd", fdump)
                    st.inc(16)
                    def ffd(s, tgt=st.v):
                        s.wait_ge(st.sem, tgt)
                    p.emit("sync", ffd)
                    p.finalize()
                    return b.nc


                r3 = lambda ap_: ap_.rearrange("p (u h) -> p u h", h=64)

                def fA2(v, c=c, a1v=a1v):
                    v.wait_ge(sA1.sem, a1v)
                    GB = b.psA[:, 2 * 512: 3 * 512]
                    v.tensor_tensor(out=r3(tscrB[:]), in0=r3(GB),
                                    in1=gsc[:, 0:8].unsqueeze(2).broadcast_to(
                                        (128, 8, 64)), op=ALU.subtract)
                    v.drain()
                    v.tensor_tensor(out=r3(gamTB[:]), in0=r3(tscrB[:]),
                                    in1=mkb3g, op=ALU.subtract)
                    v.tensor_tensor(out=r3(gamUB[:]), in0=r3(tscrB[:]),
                                    in1=mkb3l, op=ALU.add)
                    v.drain()
                    v.drain()
                    v.nop().then_inc(sA2.sem, 1)
                p.emit("vector", fA2, reads=[gsc[:], mkKrg[:], mkKrl[:], tscrB[:]],
                       writes=[tscrB[:], gamTB[:], gamUB[:]])
                a2v = sA2.inc()
                if c == 0 and stop_after == 'pa2':
                    def fdump(g, rv=a2v):
                        g.wait_ge(sA2.sem, rv)
                        g.dma_start(out=dbg.ap()[:, 0:512],
                                    in_=tscrB[:, 0:512]).then_inc(st.sem, 16)
                    p.emit("gpsimd", fdump)
                    st.inc(16)
                    def ffd(s, tgt=st.v):
                        s.wait_ge(st.sem, tgt)
                    p.emit("sync", ffd)
                    p.finalize()
                    return b.nc


                def fA2s(s, a2v=a2v):
                    s.wait_ge(sA2.sem, a2v)
                    s.activation(out=gamTB[:], in_=gamTB[:], func=AF.Exp)
                    s.drain()
                    s.activation(out=gamUB[:], in_=gamUB[:], func=AF.Exp,
                                 scale=-1.0).then_inc(sA2.sem, 1)
                p.emit("scalar", fA2s, reads=[gamTB[:], gamUB[:]],
                       writes=[gamTB[:], gamUB[:]])
                a2v2 = sA2.inc()
                if c == 0 and stop_after == 'pa2s':
                    def fdump(g, rv=a2v2):
                        g.wait_ge(sA2.sem, rv)
                        g.dma_start(out=dbg.ap()[:, 0:512],
                                    in_=gamTB[:, 0:512]).then_inc(st.sem, 16)
                    p.emit("gpsimd", fdump)
                    st.inc(16)
                    def ffd(s, tgt=st.v):
                        s.wait_ge(st.sem, tgt)
                    p.emit("sync", ffd)
                    p.finalize()
                    return b.nc


                def fA3(v, c=c, a2v2=a2v2):
                    v.wait_ge(sA2.sem, a2v2)
                    KQ = b.psA[:, 0:512]
                    KK = b.psA[:, 512:1024]
                    BB = b.psA[:, 3 * 512: 4 * 512]
                    VV = b.psA[:, 4 * 512: 5 * 512]
                    dsel = bass_rust.AP(tensor=dcolP[:].tensor, offset=c,
                                        ap=[[NU, 128], [NCH, NPR]])
                    gsrc = bass_rust.AP(tensor=gamTB[:].tensor, offset=63,
                                        ap=[[512, 128], [64, NPR]])
                    v.tensor_copy(out=dsel, in_=gsrc)
                    gqd = bass_rust.AP(tensor=Gqbuf[:].tensor, offset=c * 64,
                                       ap=[[NU * 64, 128], [NCH * 64, NPR], [1, 64]])
                    v.tensor_tensor(out=gqd, in0=r3(KQ), in1=r3(gamTB[:]), op=ALU.mult)
                    v.drain()
                    v.tensor_tensor(out=r3(tscrB[:]), in0=r3(KK), in1=r3(gamTB[:]),
                                    op=ALU.mult)
                    v.drain()
                    v.tensor_tensor(out=r3(tscrB[:]), in0=r3(tscrB[:]), in1=s01U3,
                                    op=ALU.mult)
                    v.drain()
                    v.tensor_tensor(out=r3(nbufB[:, 0:512]), in0=r3(tscrB[:]),
                                    in1=r3(BB), op=ALU.mult)
                    v.tensor_tensor(out=r3(tscrB[:]), in0=r3(KK), in1=r3(gamUB[:]),
                                    op=ALU.mult)
                    v.drain()
                    v.tensor_tensor(out=r3(tscrB[:]), in0=r3(tscrB[:]), in1=s01L3,
                                    op=ALU.mult)
                    v.drain()
                    v.tensor_tensor(out=r3(ntbufB[:, 0:512]), in0=r3(tscrB[:]),
                                    in1=gsc[:, 16:24].unsqueeze(2).broadcast_to(
                                        (128, 8, 64)), op=ALU.mult)
                    xu = bass_rust.AP(tensor=xsolB[:].tensor, offset=0,
                                      ap=[[2048, 128], [128, NPR], [1, 64]])
                    v.tensor_tensor(out=xu, in0=r3(VV),
                                    in1=gsc[:, 8:16].unsqueeze(2).broadcast_to(
                                        (128, 8, 64)), op=ALU.mult)
                    kstr = bass_rust.AP(tensor=kTc[:].tensor, offset=c * 64,
                                        ap=[[NU * 64, 128], [NCH * 64, NPR], [1, 64]])
                    xw = bass_rust.AP(tensor=xsolB[:].tensor, offset=64,
                                      ap=[[2048, 128], [128, NPR], [1, 64]])
                    v.tensor_tensor(out=xw, in0=kstr,
                                    in1=gsc[:, 24:32].unsqueeze(2).broadcast_to(
                                        (128, 8, 64)), op=ALU.mult)
                    v.drain()
                    v.nop().then_inc(sA2.sem, 1)
                p.emit("vector", fA3,
                       reads=[gamTB[:], gamUB[:], s01U[:], s01L[:], gsc[:],
                              kTc[:], tscrB[:]],
                       writes=[dcolP[:], Gqbuf[:], tscrB[:], nbufB[:], ntbufB[:],
                               xsolB[:]])
                a3v = sA2.inc()
                if c == 0 and stop_after == 'pa3':
                    def fdump(g, rv=a3v):
                        g.wait_ge(sA2.sem, rv)
                        g.dma_start(out=dbg.ap()[:, 0:2048],
                                    in_=xsolB[:, 0:2048]).then_inc(st.sem, 16)
                    p.emit("gpsimd", fdump)
                    st.inc(16)
                    def ffd(s, tgt=st.v):
                        s.wait_ge(st.sem, tgt)
                    p.emit("sync", ffd)
                    p.finalize()
                    return b.nc


                xcur = 0
                for lev in range(6):
                    def fLm(t, lev=lev, xcur=xcur, a3v=a3v, a4v=sA4.v):
                        if lev == 0:
                            t.wait_ge(sA2.sem, a3v)
                        else:
                            t.wait_ge(sA4.sem, a4v)
                        xb = 0 if lev % 2 == 0 else 4 * 512
                        sb_ = 2 * 512 if lev % 2 == 0 else 0
                        do_x = not (stop_after == 'plv2s' and lev == 2)
                        do_s = lev < 5 and not (stop_after == 'plv2x' and lev == 2)
                        mm = None
                        for pr in range(NPR):
                            for hh in range(2):
                                sl = slice(hh * 64, hh * 64 + 64)
                                nsl = nbufB[sl, (xcur % 2) * 512 + pr * 64:
                                            (xcur % 2) * 512 + pr * 64 + 64]
                                ntsl = ntbufB[sl, (xcur % 2) * 512 + pr * 64:
                                              (xcur % 2) * 512 + pr * 64 + 64]
                                xsl = xsolB[sl, (xcur % 2) * 1024 + pr * 128:
                                            (xcur % 2) * 1024 + pr * 128 + 128]
                                if do_x:
                                    t.matmul(b.psA[sl, xb + pr * 128: xb + pr * 128 + 128],
                                             nsl, xsl, start=True, stop=False)
                                    mm = t.matmul(b.psA[sl, xb + pr * 128: xb + pr * 128 + 128],
                                                  b.ident[sl, sl], xsl,
                                                  start=False, stop=True)
                                if do_s:
                                    t.matmul(b.psA[sl, sb_ + pr * 64: sb_ + pr * 64 + 64],
                                             ntsl, nsl, start=True, stop=True)
                                    mm = t.matmul(b.psA[sl, sb_ + 512 + pr * 64:
                                                        sb_ + 512 + pr * 64 + 64],
                                                  nsl, ntsl, start=True, stop=True)
                        mm.then_inc(sA3.sem, 1)
                    p.emit("tensor", fLm)
                    a3mv = sA3.inc()

                    def fLeS(s, lev=lev, xcur=xcur, a3mv=a3mv):
                        s.wait_ge(sA3.sem, a3mv)
                        xb = 0 if lev % 2 == 0 else 4 * 512
                        nxt = (xcur + 1) % 2
                        s.activation(out=xsolB[:, nxt * 1024: nxt * 1024 + 512],
                                     in_=b.psA[:, xb: xb + 512], func=AF.Copy)
                        s.activation(out=xsolB[:, nxt * 1024 + 512:(nxt + 1) * 1024],
                                     in_=b.psA[:, xb + 512: xb + 1024],
                                     func=AF.Copy).then_inc(sA4.sem, 1)
                    p.emit("scalar", fLeS, writes=[xsolB[:]])
                    sA4.inc()
                    if lev < 5:
                        def fLe(v, lev=lev, xcur=xcur, a3mv=a3mv):
                            v.wait_ge(sA3.sem, a3mv)
                            sb_ = 2 * 512 if lev % 2 == 0 else 0
                            nxt = (xcur + 1) % 2
                            v.tensor_copy(out=nbufB[:, nxt * 512:(nxt + 1) * 512],
                                          in_=b.psA[:, sb_: sb_ + 512])
                            v.tensor_copy(out=ntbufB[:, nxt * 512:(nxt + 1) * 512],
                                          in_=b.psA[:, sb_ + 512: sb_ + 1024])
                            v.drain()
                            v.nop().then_inc(sA4.sem, 1)
                        p.emit("vector", fLe, writes=[nbufB[:], ntbufB[:]])
                        sA4.inc()
                    if c == 0 and (stop_after == 'plv%d' % lev or
                                   (lev == 2 and stop_after in ('plv2x', 'plv2s'))):
                        def fdump(g, rv=sA4.v):
                            g.wait_ge(sA4.sem, rv)
                            g.dma_start(out=dbg.ap()[:, 0:2048],
                                        in_=xsolB[:, 0:2048]).then_inc(st.sem, 16)
                        p.emit("gpsimd", fdump)
                        st.inc(16)
                        def ffd(s, tgt=st.v):
                            s.wait_ge(st.sem, tgt)
                        p.emit("sync", ffd)
                        p.finalize()
                        return b.nc

                    xcur += 1

                def fA5(v, c=c, xcur=xcur, a4v=sA4.v):
                    v.wait_ge(sA4.sem, a4v)
                    ud = bass_rust.AP(tensor=Ubuf[:].tensor, offset=c * 64,
                                      ap=[[NU * 64, 128], [NCH * 64, NPR], [1, 64]])
                    us = bass_rust.AP(tensor=xsolB[:].tensor,
                                      offset=(xcur % 2) * 1024,
                                      ap=[[2048, 128], [128, NPR], [1, 64]])
                    v.tensor_copy(out=ud, in_=us)
                    v.drain()
                    v.nop().then_inc(sA2.sem, 1)
                p.emit("vector", fA5, reads=[xsolB[:]], writes=[Ubuf[:]])
                a5v = sA2.inc()
                if c == 0 and stop_after == 'pa5':
                    def fdump(g, rv=a5v):
                        g.wait_ge(sA2.sem, rv)
                        g.dma_start(out=dbg.ap()[:, 0:4096],
                                    in_=Ubuf[:, 0:4096]).then_inc(st.sem, 16)
                    p.emit("gpsimd", fdump)
                    st.inc(16)
                    def ffd(s, tgt=st.v):
                        s.wait_ge(st.sem, tgt)
                    p.emit("sync", ffd)
                    p.finalize()
                    return b.nc


                def fA5t(t, xcur=xcur, a5v=a5v):
                    t.wait_ge(sA2.sem, a5v)
                    mm = None
                    for pr in range(NPR):
                        for hh in range(2):
                            sl = slice(hh * 64, hh * 64 + 64)
                            mm = t.matmul(
                                b.psA[sl, 2 * 512 + pr * 64: 2 * 512 + pr * 64 + 64],
                                xsolB[sl, (xcur % 2) * 1024 + pr * 128 + 64:
                                      (xcur % 2) * 1024 + pr * 128 + 128],
                                b.ident[sl, sl], start=True, stop=True)
                    mm.then_inc(sA3.sem, 1)
                p.emit("tensor", fA5t)
                a5tv = sA3.inc()

                def fA5e(v, c=c, a5tv=a5tv):
                    v.wait_ge(sA3.sem, a5tv)
                    wd = bass_rust.AP(tensor=WTbuf[:].tensor, offset=c * 64,
                                      ap=[[NU * 64, 128], [NCH * 64, NPR], [1, 64]])
                    v.tensor_copy(out=wd,
                                  in_=b.psA[:, 2 * 512: 3 * 512].rearrange(
                                      "p (u h) -> p u h", h=64))
                    v.drain()
                    v.nop().then_inc(sA2.sem, 1)
                p.emit("vector", fA5e, writes=[WTbuf[:]])
                sA2.inc()
            SA_ALL = sA2.v
            if stop_after == 'pha':
                def fdump(g, rv=SA_ALL):
                    g.wait_ge(sA2.sem, rv)
                    g.dma_start(out=dbg.ap()[:, 0:4096],
                                in_=Ubuf[:, 0:4096]).then_inc(st.sem, 16)
                p.emit("gpsimd", fdump)
                st.inc(16)
                def ffd(s, tgt=st.v):
                    s.wait_ge(st.sem, tgt)
                p.emit("sync", ffd)
                p.finalize()
                return b.nc


            for c in range(NCH):
                def fB0(v, c=c, b3p=sB3.v, b4p=sB4.v):
                    if b3p > 0:
                        v.wait_ge(sB3.sem, b3p)
                    if b4p > 0:
                        v.wait_ge(sB4.sem, b4p)
                    for k_, srcT in enumerate((gacolP, dcolP)):
                        src = bass_rust.AP(tensor=srcT[:].tensor, offset=c,
                                           ap=[[NU, 128], [NCH, NPR]])
                        v.tensor_copy(out=gsc2[:, k_ * 8:(k_ + 1) * 8], in_=src)
                    v.drain()
                    v.tensor_tensor(out=gccB[:], in0=gsc2[:, 0:8],
                                    in1=gsc2[:, 8:16], op=ALU.mult)
                    v.drain()
                    v.tensor_tensor(out=r3b(s0sB[:]), in0=r3b(s0f[:]),
                                    in1=gccB[:].unsqueeze(2).broadcast_to(
                                        (128, NPR, 64)), op=ALU.mult)
                    v.drain()
                    v.nop().then_inc(sB1.sem, 1)
                p.emit("vector", fB0, reads=[gacolP[:], dcolP[:], s0f[:], gccB[:],
                                             gsc2[:]],
                       writes=[gccB[:], s0sB[:], gsc2[:]])
                b1v = sB1.inc()

                def fB1(t, c=c, b1v=b1v, saall=SA_ALL, b2p=sB2.v):
                    t.wait_ge(sB1.sem, b1v)
                    if c == 0:
                        t.wait_ge(sA2.sem, saall)
                    if b2p > 0:
                        t.wait_ge(sB2.sem, b2p)
                    mm = None
                    for pr in range(NPR):
                        u = pr * NCH + c
                        qts = q_cm[:, pr * BT + c * 64: pr * BT + c * 64 + 64]
                        for hh in range(2):
                            sl = slice(hh * 64, hh * 64 + 64)
                            t.matmul(b.psA[sl, pr * 64: pr * 64 + 64],
                                     WTbuf[sl, u * 64:(u + 1) * 64],
                                     s0b[sl, pr * 64:(pr + 1) * 64],
                                     start=True, stop=False)
                            t.matmul(b.psA[sl, pr * 64: pr * 64 + 64],
                                     b.ident[sl, sl],
                                     Ubuf[sl, u * 64:(u + 1) * 64],
                                     start=False, stop=True)
                            mm = t.matmul(b.psA[sl, 512 + pr * 64: 512 + pr * 64 + 64],
                                          qts[sl, :], s0b[sl, pr * 64:(pr + 1) * 64],
                                          start=True, stop=True)
                    mm.then_inc(sB1.sem, 1)
                p.emit("tensor", fB1)
                b1v2 = sB1.inc()

                def fB2(v, c=c, b1v2=b1v2):
                    v.wait_ge(sB1.sem, b1v2)
                    v.tensor_copy(out=zhatB[:], in_=b.psA[:, 0:512])
                    v.drain()
                    v.tensor_tensor(out=r3b(zprB[:]), in0=r3b(zhatB[:]),
                                    in1=gsc2[:, 8:16].unsqueeze(2).broadcast_to(
                                        (128, NPR, 64)), op=ALU.mult)
                    v.tensor_tensor(out=r3b(rt1B[:]), in0=r3b(b.psA[:, 512:1024]),
                                    in1=gsc2[:, 0:8].unsqueeze(2).broadcast_to(
                                        (128, NPR, 64)), op=ALU.mult)
                    v.drain()
                    v.nop().then_inc(sB2.sem, 1)
                p.emit("vector", fB2, reads=[gsc2[:], zhatB[:]],
                       writes=[zhatB[:], zprB[:], rt1B[:]])
                b2v = sB2.inc()

                def fB3(t, c=c, b2v=b2v, b4p=sB4.v):
                    t.wait_ge(sB2.sem, b2v)
                    if b4p > 0:
                        t.wait_ge(sB4.sem, b4p)
                    mm = None
                    for pr in range(NPR):
                        u = pr * NCH + c
                        for hh in range(2):
                            sl = slice(hh * 64, hh * 64 + 64)
                            t.matmul(b.psA[sl, 2 * 512 + pr * 64: 2 * 512 + pr * 64 + 64],
                                     Gqbuf[sl, u * 64:(u + 1) * 64],
                                     zhatB[sl, pr * 64:(pr + 1) * 64],
                                     start=True, stop=True)
                            t.matmul(b.psA[sl, 3 * 512 + pr * 64: 3 * 512 + pr * 64 + 64],
                                     kTc[sl, u * 64:(u + 1) * 64],
                                     zprB[sl, pr * 64:(pr + 1) * 64],
                                     start=True, stop=False)
                            mm = t.matmul(b.psA[sl, 3 * 512 + pr * 64: 3 * 512 + pr * 64 + 64],
                                          identf[sl, sl], s0sB[sl, pr * 64:(pr + 1) * 64],
                                          start=False, stop=True)
                    mm.then_inc(sB3.sem, 1)
                p.emit("tensor", fB3)
                b3v = sB3.inc()

                def fB4(v, c=c, b3v=b3v):
                    v.wait_ge(sB3.sem, b3v)
                    rdst = bass_rust.AP(tensor=rtm[:].tensor, offset=c * 64,
                                        ap=[[NU * 64, 128], [NCH * 64, NPR], [1, 64]])
                    v.scalar_tensor_tensor(out=rdst, in0=r3b(rt1B[:]), scalar=1.0,
                                           in1=b.psA[:, 2 * 512: 3 * 512].rearrange(
                                               "p (u h) -> p u h", h=64),
                                           op0=ALU.mult, op1=ALU.add)
                    v.tensor_copy(out=s0f[:], in_=b.psA[:, 3 * 512: 4 * 512])
                    v.drain()
                    v.tensor_copy(out=s0b[:], in_=s0f[:])
                    v.drain()
                    v.nop().then_inc(sB4.sem, 1)
                p.emit("vector", fB4, reads=[rt1B[:], s0f[:]],
                       writes=[rtm[:], s0f[:], s0b[:]])
                sB4.inc()
            SB_ALL = sB4.v
            if stop_after == 'phb':
                def fdump(g, rv=SB_ALL):
                    g.wait_ge(sB4.sem, rv)
                    g.dma_start(out=dbg.ap()[:, 0:4096],
                                in_=rtm[:, 0:4096]).then_inc(st.sem, 16)
                p.emit("gpsimd", fdump)
                st.inc(16)
                def ffd(s, tgt=st.v):
                    s.wait_ge(st.sem, tgt)
                p.emit("sync", ffd)
                p.finalize()
                return b.nc


            for g_ in range(8):
                def fsqr(s, g_=g_, sball=SB_ALL, rsp=rsd.v):
                    if g_ == 0:
                        s.wait_ge(sB4.sem, sball)
                    else:
                        s.wait_ge(rsd.sem, rsp)
                    s.activation(out=cbuf[:, 0:512],
                                 in_=rtm[:, g_ * 512:(g_ + 1) * 512],
                                 func=AF.Square).then_inc(rsd.sem, 1)
                p.emit("scalar", fsqr, reads=[rtm[:]], writes=[cbuf[:]])
                sv1 = rsd.inc()

                def frd(v, g_=g_, sv1=sv1):
                    v.wait_ge(rsd.sem, sv1)
                    v.tensor_reduce(out=rsq[:, g_ * 8:(g_ + 1) * 8],
                                    in_=cbuf[:, 0:512].rearrange(
                                        "p (u h) -> p u h", h=64),
                                    axis=mybir.AxisListType.X,
                                    op=ALU.add).then_inc(rsd.sem, 1)
                p.emit("vector", frd, reads=[cbuf[:]], writes=[rsq[:]])
                rsd.inc()
            RSQ_ALL = rsd.v

            def frd1(v, rall=RSQ_ALL):
                v.wait_ge(rsd.sem, rall)
                v.tensor_scalar(out=rsq[:], in0=rsq[:], scalar1=1.0 / 64,
                                scalar2=EPS, op0=ALU.mult,
                                op1=ALU.add).then_inc(rsd.sem, 1)
            p.emit("vector", frd1, reads=[rsq[:]], writes=[rsq[:]])
            rd1 = rsd.inc()

            def frd2(s, rd1=rd1):
                s.wait_ge(rsd.sem, rd1)
                s.activation(out=rsq[:], in_=rsq[:], func=AF.Ln)
                s.drain()
                s.activation(out=rsq[:], in_=rsq[:], func=AF.Exp,
                             scale=-0.5).then_inc(rsd.sem, 1)
            p.emit("scalar", frd2, reads=[rsq[:]], writes=[rsq[:]])
            rd2 = rsd.inc()

            def fra(v, rd2=rd2):
                v.wait_ge(rsd.sem, rd2)
                rb = rsq[:].unsqueeze(2).broadcast_to((128, NU, 64))
                v.tensor_tensor(out=rtm[:].rearrange("p (u h) -> p u h", h=64),
                                in0=rtm[:].rearrange("p (u h) -> p u h", h=64),
                                in1=rb, op=ALU.mult)
                v.drain()
                nb_ = nwb[:].unsqueeze(1).broadcast_to((128, NU, 64))
                v.tensor_tensor(out=rtm[:].rearrange("p (u h) -> p u h", h=64),
                                in0=rtm[:].rearrange("p (u h) -> p u h", h=64),
                                in1=nb_, op=ALU.mult)
                v.drain()
                v.nop().then_inc(rse.sem, 1)
            p.emit("vector", fra, reads=[rtm[:], rsq[:], nwb[:]], writes=[rtm[:]])
            rav = rse.inc()

            for c in range(NCH):
                def fnt(t, c=c, rav=rav, ntev=nte.v):
                    if c == 0:
                        t.wait_ge(rse.sem, rav)
                    if c >= 2:
                        t.wait_ge(nte.sem, ntev - 1)
                    bk = 4 if c % 2 == 0 else 5
                    mm = None
                    for pr in range(NPR):
                        u = pr * NCH + c
                        for hh in range(2):
                            sl = slice(hh * 64, hh * 64 + 64)
                            mm = t.matmul(b.psA[sl, bk * 512 + pr * 64:
                                                bk * 512 + pr * 64 + 64],
                                          rtm[sl, u * 64:(u + 1) * 64],
                                          b.ident[sl, sl], start=True, stop=True)
                    mm.then_inc(ntr.sem, 1)
                p.emit("tensor", fnt)
                nv = ntr.inc()

                def fne(v, c=c, nv=nv):
                    v.wait_ge(ntr.sem, nv)
                    bk = 4 if c % 2 == 0 else 5
                    for hh in range(2):
                        src = bass_rust.AP(
                            tensor=b.psA[:].tensor,
                            offset=(hh * 64) * 4096 + bk * 512,
                            ap=[[4096, 64], [64, NPR], [1, 64]])
                        dst = bass_rust.AP(
                            tensor=ncm[:].tensor,
                            offset=(hh * 64) * (NKT * BT) + c * 64,
                            ap=[[NKT * BT, 64], [BT, NPR], [1, 64]])
                        v.tensor_copy(out=dst, in_=src)
                    v.drain()
                    v.nop().then_inc(nte.sem, 1)
                p.emit("vector", fne, writes=[ncm[:]])
                nte.inc()

            NTE_ALL = nte.v
            if stop_after == 'nte':
                def fdump(g, rv=NTE_ALL):
                    g.wait_ge(nte.sem, rv)
                    g.dma_start(out=dbg.ap()[:, 0:4096],
                                in_=ncm[:, 0:4096]).then_inc(st.sem, 16)
                p.emit("gpsimd", fdump)
                st.inc(16)
                def ffd(s, tgt=st.v):
                    s.wait_ge(st.sem, tgt)
                p.emit("sync", ffd)
                p.finalize()
                return b.nc


            for tb in range(NTB):
                tt = bc * NTB + tb
                tt2 = tt // 4
                for kt in range(NKT):
                    def flr(g, kt=kt, tt=tt):
                        if kt == 0 and tt >= 2:
                            g.wait_ge(owe.sem, (tt - 1) * NKT)
                        g.dma_start(out=xres[:, (tt % 2) * NKT * 128 + kt * 128:
                                            (tt % 2) * NKT * 128 + kt * 128 + 128],
                                    in_=x_cm.ap()[kt * 128:(kt + 1) * 128,
                                                  tt * 128:tt * 128 + 128]).then_inc(ldr.sem, 16)
                    p.emit("gpsimd", flr)
                    ldr.inc(16)
                lrv = ldr.v

                def fow(t, tb=tb, tt=tt, lrv=lrv, nteall=NTE_ALL):
                    t.wait_ge(nte.sem, nteall)
                    t.wait_ge(ldr.sem, lrv)
                    if tt >= 2:
                        t.wait_ge(owe.sem, (tt - 1) * NKT)
                    mm = None
                    for oc in range(NKT):
                        bank = (4 if tt % 2 == 0 else 2) + oc // 4
                        coff = (oc % 4) * 128
                        for kt in range(NKT):
                            mm = t.matmul(b.psA[:, bank * 512 + coff:
                                                bank * 512 + coff + 128],
                                          outw_s[:, kt * D + oc * 128: kt * D + oc * 128 + 128],
                                          ncm[:, kt * BT + tb * 128: kt * BT + tb * 128 + 128],
                                          start=(kt == 0), stop=(kt == NKT - 1))
                    mm.then_inc(owm.sem, 1)
                p.emit("tensor", fow)
                owv = owm.inc()

                for oc in range(NKT):
                    def fwe(v, oc=oc, tb=tb, tt=tt, tt2=tt2, owv=owv):
                        if oc == 0:
                            v.wait_ge(owm.sem, owv)
                            if tt % 4 == 0 and tt2 >= 2:
                                v.wait_ge(pln['ap_done'].sem, (tt2 - 1) * NKT)
                                v.wait_ge(pln['sq_cons'].sem, (tt2 - 1) * NKT)
                        bank = (4 if tt % 2 == 0 else 2) + oc // 4
                        coff = (oc % 4) * 128
                        v.tensor_scalar(out=cbuf2[:, 0:128],
                                        in0=b.psA[:, bank * 512 + coff:
                                                  bank * 512 + coff + 128],
                                        scalar1=outb_s[:, oc:oc + 1], scalar2=None,
                                        op0=ALU.add)
                        v.drain()
                        v.tensor_tensor(out=cbuf2[:, 0:128], in0=cbuf2[:, 0:128],
                                        in1=gsig[:, oc * BT + tb * 128:
                                                 oc * BT + tb * 128 + 128],
                                        op=ALU.mult)
                        v.drain()
                        v.tensor_tensor(
                            out=rsl[:, (tt2 % 2) * NKT * 512 + oc * 512 + (tt % 4) * 128:
                                    (tt2 % 2) * NKT * 512 + oc * 512 + (tt % 4) * 128 + 128],
                            in0=cbuf2[:, 0:128],
                            in1=xres[:, (tt % 2) * NKT * 128 + oc * 128:
                                     (tt % 2) * NKT * 128 + oc * 128 + 128],
                            op=ALU.add).then_inc(owe.sem, 1)
                    p.emit("vector", fwe, reads=[gsig[:], xres[:], cbuf2[:], outb_s[:]],
                           writes=[cbuf2[:], rsl[:]])
                    owe.inc()
                if tt % 4 == 3:
                    pl.slice(tt // 4)

        if standalone:
            def ffinal(s, target=st.v):
                s.wait_ge(st.sem, target)
            p.emit("sync", ffinal)
            p.finalize()
        else:
            emit_phase_barrier(b, st)
    return b.nc


def stage_l1attn(inputs):
    qkvw = np.asarray(inputs["L1_attn_qkv_w"], np.float32)
    ctrlw = np.asarray(inputs["L1_attn_ctrl_w"], np.float32)
    ctrlb = np.asarray(inputs["L1_attn_ctrl_b"], np.float32)
    ctrlpT = np.zeros((D, 1152), np.float32)
    ctrlpT[:, 0:16] = ctrlw[0:16].T
    ctrlpT[:, 64:80] = ctrlw[16:32].T
    ctrlpT[:, 128:1152] = ctrlw[32:1056].T
    ctrlbc = np.zeros((128, 9), np.float32)
    ctrlbc[0:16, 0] = ctrlb[0:16]
    ctrlbc[64:80, 0] = ctrlb[16:32]
    ctrlbc[:, 1:9] = ctrlb[32:1056].reshape(8, 128).T
    cw = np.zeros((128, NKT * 12), np.float32)
    for kt in range(NKT):
        for ti, nm in enumerate(["L1_attn_convq_w", "L1_attn_convk_w",
                                 "L1_attn_convv_w"]):
            w = np.asarray(inputs[nm], np.float32)
            cw[:, kt * 12 + ti * 4: kt * 12 + ti * 4 + 4] = w[kt * 128:(kt + 1) * 128, :]
    nw = np.asarray(inputs["L1_attn_norm_w"], np.float32)
    nwt = np.tile(nw.reshape(1, 64), (128, 1)).copy()
    cst2 = np.zeros((128, 1536), np.float32)
    r = np.arange(64)
    rr, cc = np.meshgrid(r, r, indexing='ij')
    for half in range(2):
        sl = slice(half * 64, half * 64 + 64)
        cst2[sl, 0:64] = np.where(rr > cc, 1e30, 0.0)
        cst2[sl, 64:128] = np.where(rr < cc, 1e30, 0.0)
        cst2[sl, 128:192] = np.where(rr < cc, -1.0, 0.0)
        cst2[sl, 192:256] = np.where(rr > cc, 1.0, 0.0)
    hs = np.zeros((128, 128), np.float32)
    hstv = np.zeros((16, 1024), np.float32)
    for kt in range(8):
        for d_ in range(128):
            hs[d_, kt * 16 + 2 * kt + d_ // 64] = 1.0
            hstv[2 * kt + d_ // 64, kt * 128 + d_] = 1.0
    cst2[:, 256:384] = hs
    cst2[0:16, 384:1408] = hstv
    cst2[:, 1408:1536] = np.eye(128)
    return {
        "qkvwT": np.ascontiguousarray(qkvw.T),
        "ctrlpT": ctrlpT, "ctrlbc": ctrlbc, "convw": cw, "nwtile": nwt,
        "outwT": np.ascontiguousarray(np.asarray(inputs["L1_attn_out_w"], np.float32).T),
        "outb": np.asarray(inputs["L1_attn_out_b"], np.float32).reshape(-1, 1).copy(),
        "cst2": cst2, "consts": host_consts(),
    }
